# version 48
# baseline (speedup 1.0000x reference)
import numpy as np
from collections import defaultdict
from contextlib import ExitStack

import concourse.bass as bass
import concourse.mybir as mybir
from concourse.bass_utils import run_bass_kernel_spmd

F32 = mybir.dt.float32
BF16 = mybir.dt.bfloat16
AF = mybir.ActivationFunctionType
ALU = mybir.AluOpType
AX = mybir.AxisListType
f32 = np.float32

N_CORES = 8
TOK = 2048
NT = TOK // 128
NCH = TOK // 512
D = 1024
ND = D // 128
H = 16
NP = 2048
NPC = NP // 128
EL = 8
C1 = 3.0
C2 = 4.0
LNEPS = 1e-5
RG = [list(range(N_CORES))]

ENGINES = ("pe", "act", "dve", "pool", "sp")
DMA_POOLS = {"sp": [f"ld{i}" for i in range(6)], "pool": [f"po{i}" for i in range(6)]}
PSUM_BANK = {"ps_cx": 512, "ps_aux": 512, "ps_tr": 1024, "ps_big": 512}


class MKB:

    def __init__(self):
        self.plans = {e: [] for e in ENGINES}
        self.counts = defaultdict(int)
        self.waited = {e: defaultdict(int) for e in ENGINES}
        self.w_recs = defaultdict(list)
        self.r_recs = defaultdict(list)
        self.dma_rr = {"sp": 0, "pool": 0}
        self.pending_barrier = {e: None for e in ENGINES}
        self.sem_names = set(ENGINES) | {s for p in DMA_POOLS.values() for s in p} | {"cc"}

    def _deps(self, eng, reads, writes):
        need = {}

        def add(sem, val, weng):
            if weng == "pe" and eng == "pe":
                return
            if self.waited[eng].get(sem, 0) >= val:
                return
            if need.get(sem, 0) < val:
                need[sem] = val

        def overlap(a, b):
            return a[0] < b[1] and b[0] < a[1] and a[2] < b[3] and b[2] < a[3]

        for r in reads:
            key, rng = r[0], r[1:]
            for rec in self.w_recs[key]:
                if overlap(rng, rec[:4]):
                    add(rec[4], rec[5], rec[6])
        for w in writes:
            key, rng = w[0], w[1:]
            for rec in self.w_recs[key]:
                if overlap(rng, rec[:4]):
                    add(rec[4], rec[5], rec[6])
            for rec in self.r_recs[key]:
                if overlap(rng, rec[:4]):
                    add(rec[4], rec[5], rec[6])
        bar = self.pending_barrier[eng]
        if bar is not None:
            for s, v in bar.items():
                if v > 0 and self.waited[eng].get(s, 0) < v:
                    need[s] = max(need.get(s, 0), v)
            self.pending_barrier[eng] = None
        for s, v in need.items():
            self.waited[eng][s] = max(self.waited[eng].get(s, 0), v)
        return need

    def _record(self, eng, sem, val, reads, writes):
        def covers(a, b):
            return a[0] <= b[0] and a[1] >= b[1] and a[2] <= b[2] and a[3] >= b[3]
        for w in writes:
            key, rng = w[0], w[1:]
            self.w_recs[key] = [x for x in self.w_recs[key] if not covers(rng, x[:4])]
            self.w_recs[key].append((*rng, sem, val, eng))
            self.r_recs[key] = [x for x in self.r_recs[key] if not covers(rng, x[:4])]
        for r in reads:
            key, rng = r[0], r[1:]
            self.r_recs[key] = [x for x in self.r_recs[key] if not covers(rng, x[:4])]
            self.r_recs[key].append((*rng, sem, val, eng))

    @staticmethod
    def _norm(reads, writes):
        def ex(item):
            key, p0, p1, e0, e1 = item
            b = PSUM_BANK.get(key)
            if b:
                e0 = (e0 // b) * b
                e1 = ((e1 + b - 1) // b) * b
                return (key, 0, 128, e0, e1), True
            return (key, p0, p1, e0, e1), False
        nr, nw = [], list()
        for w in writes:
            nw.append(ex(w)[0])
        for r in reads:
            item, is_psum = ex(r)
            (nw if is_psum else nr).append(item)
        return nr, nw

    def op(self, eng, fn, reads=(), writes=()):
        reads, writes = self._norm(reads, writes)
        need = self._deps(eng, reads, writes)
        self.counts[eng] += 1
        v = self.counts[eng]
        self.plans[eng].append((need, fn, eng, 1))
        self._record(eng, eng, v, reads, writes)

    def dma(self, qeng, fn, reads=(), writes=()):
        reads, writes = self._norm(reads, writes)
        pool = DMA_POOLS[qeng]
        sem = pool[self.dma_rr[qeng] % len(pool)]
        self.dma_rr[qeng] += 1
        need = self._deps(qeng, reads, writes)
        prev = self.counts[sem]
        if prev > 0 and self.waited[qeng].get(sem, 0) < prev:
            need[sem] = max(need.get(sem, 0), prev)
            self.waited[qeng][sem] = max(self.waited[qeng].get(sem, 0), prev)
        self.counts[sem] = prev + 16
        self.plans[qeng].append((need, fn, sem, 16))
        self._record("dma", sem, prev + 16, reads, writes)

    def collective(self, fn, reads=(), writes=()):
        need = self._deps("pool", reads, writes)
        self.counts["cc"] += 1
        v = self.counts["cc"]
        self.plans["pool"].append((need, fn, "cc", 1))
        self._record("cc", "cc", v, reads, writes)

    def barrier(self, exclude=()):
        snap = {s: self.counts[s] for s in list(self.counts) if s not in exclude}
        for e in ENGINES:
            pb = self.pending_barrier[e]
            if pb is not None:
                for s, v in snap.items():
                    pb[s] = max(pb.get(s, 0), v)
            else:
                self.pending_barrier[e] = dict(snap)
        if exclude:
            for recs in (self.w_recs, self.r_recs):
                for k in list(recs):
                    kept = [x for x in recs[k] if x[4] in exclude]
                    if kept:
                        recs[k] = kept
                    else:
                        del recs[k]
        else:
            self.w_recs.clear()
            self.r_recs.clear()

    def wait_all(self, eng, sems):
        need = {s: self.counts[s] for s in sems if self.counts[s] > 0}
        self.plans[eng].append((need, None, None, 0))

    def replay(self, eng, handle, sem_handles):
        for need, fn, sem, amt in self.plans[eng]:
            for s in sorted(need):
                handle.wait_ge(sem_handles[s], need[s])
            if fn is not None:
                inst = fn(handle)
                if sem is not None:
                    inst.then_inc(sem_handles[sem], amt)


def build(taps=()):
    nc = bass.Bass("TRN2", target_bir_lowering=False)
    mk = MKB()
    taps = set(taps)

    dx = nc.dram_tensor("x", [TOK, D], F32, kind="ExternalInput")
    dxh = nc.dram_tensor("xh", [512, D], F32, kind="ExternalInput")
    dwq = nc.dram_tensor("wqT", [D, D], F32, kind="ExternalInput")
    dwk = nc.dram_tensor("wkT", [D, D], F32, kind="ExternalInput")
    dwv = nc.dram_tensor("wvT", [D, D], F32, kind="ExternalInput")
    dwo = nc.dram_tensor("woT", [D, D], F32, kind="ExternalInput")
    dphi = nc.dram_tensor("phiT", [D, NP], F32, kind="ExternalInput")
    dbq = nc.dram_tensor("bqf", [128, ND], F32, kind="ExternalInput")
    dbk = nc.dram_tensor("bkf", [128, ND], F32, kind="ExternalInput")
    dbv = nc.dram_tensor("bvr", [1, D], F32, kind="ExternalInput")
    dbo = nc.dram_tensor("bor", [1, D], F32, kind="ExternalInput")
    dphib = nc.dram_tensor("phibr", [1, NP], F32, kind="ExternalInput")
    dkb = nc.dram_tensor("kb", [128, 1], F32, kind="ExternalInput")
    dew = nc.dram_tensor("ew", [EL * D, D], F32, kind="ExternalInput")
    deb = nc.dram_tensor("eb", [EL, D], F32, kind="ExternalInput")
    dident = nc.dram_tensor("ident", [128, 128], F32, kind="ExternalInput")
    do = nc.dram_tensor("o", [TOK, D], F32, kind="ExternalOutput")

    x2_d = nc.dram_tensor("x2_d", [TOK, D], F32, kind="Internal")
    xp_d = nc.dram_tensor("xp_d", [NP, D], F32, kind="Internal")
    xt_d = nc.dram_tensor("xt_d", [NP // N_CORES, D], F32, kind="Internal")
    yg_in = nc.dram_tensor("yg_in", [NP // N_CORES, D], BF16, kind="Internal")
    yg_out = nc.dram_tensor("yg_out", [NP, D], BF16, kind="Internal")
    ctx_d = nc.dram_tensor("ctx_d", [D, TOK], BF16, kind="Internal")

    tap_d = {}

    def mktap(name, shape, dtype=F32):
        if name in taps:
            tap_d[name] = nc.dram_tensor("tap_" + name, shape, dtype, kind="ExternalOutput")

    mktap("xnfm", [128, 4096]); mktap("kf", [128, 4096]); mktap("vt", [128, 4160])
    mktap("qf", [128, 4096]); mktap("pb", [128, 4096]); mktap("ctx", [128, NCH * 4096])
    mktap("x2", [TOK, D]); mktap("E", [128, 2048]); mktap("rP", [128, NT * 64])
    mktap("rN", [128, NT * 32]); mktap("xp", [NP, D]); mktap("xtn", [256, D])
    mktap("ytl", [NP, D])

    def tap(name, buf, p0, p1, e0, e1, dst_r0=0, key=None):
        if name not in taps:
            return
        t = tap_d[name]
        rows = p1 - p0

        def fn(h, buf=buf, t=t, p0=p0, p1=p1, e0=e0, e1=e1, dst_r0=dst_r0):
            return h.dma_start(out=t[dst_r0:dst_r0 + rows, 0:(e1 - e0)], in_=buf[p0:p1, e0:e1])
        mk.dma("pool", fn, reads=[(key or buf.name, p0, p1, e0, e1)],
               writes=[("tap_" + name, dst_r0, dst_r0 + rows, 0, e1 - e0)])

    st_outer = ExitStack()

    def sb(name, shape, dtype, stack, side="right"):
        return stack.enter_context(nc.sbuf_tensor(name, shape, dtype, side=side))

    ident_b = sb("ident_b", [128, 128], BF16, st_outer, "left")
    ones_b = sb("ones_b", [1, 128], BF16, st_outer, "left")
    ident_f = sb("ident_f", [128, 128], F32, st_outer, "left")
    rows_b = sb("rows_b", [1, 2048], BF16, st_outer, "left")
    kb_s = sb("kb_s", [128, 1], F32, st_outer, "left")
    bqf = sb("bqf_s", [128, ND], F32, st_outer, "left")
    bkf = sb("bkf_s", [128, ND], F32, st_outer, "left")
    lns = sb("lns", [128, 48], F32, st_outer, "left")
    rrow = sb("rrow", [1, 1024], BF16, st_outer, "left")
    lnr = sb("lnr", [1, 1024], F32, st_outer, "left")

    ps_st = ExitStack()
    ps_cx = ps_st.enter_context(nc.psum_tensor("ps_cx", [128, 1024], F32))
    ps_aux = ps_st.enter_context(nc.psum_tensor("ps_aux", [128, 512], F32))
    ps_tr = ps_st.enter_context(nc.psum_tensor("ps_tr", [128, 1024], BF16))
    ps_big = ps_st.enter_context(nc.psum_tensor("ps_big", [128, 2048], F32))

    def dma_in(dst, dsl, src, ssl, key, q="sp"):
        def fn(h):
            return h.dma_start(out=dst[dsl[0]:dsl[1], dsl[2]:dsl[3]],
                               in_=src[ssl[0]:ssl[1], ssl[2]:ssl[3]])
        mk.dma(q, fn, reads=[(key, *ssl)], writes=[(dst.name, *dsl)])

    def cdma(dst, dsl, srct, ssl, key):
        def fn(h):
            return h.dma_start(out=dst[dsl[0]:dsl[1], dsl[2]:dsl[3]],
                               in_=srct[ssl[0]:ssl[1], ssl[2]:ssl[3]])
        mk.dma("pool", fn, reads=[(key, *ssl)], writes=[(dst.name, *dsl)])

    def cast(dst, dsl, src, ssl, eng="dve"):
        def fn(h):
            if eng == "act":
                return h.copy(dst[dsl[0]:dsl[1], dsl[2]:dsl[3]],
                              src[ssl[0]:ssl[1], ssl[2]:ssl[3]])
            return h.tensor_copy(dst[dsl[0]:dsl[1], dsl[2]:dsl[3]],
                                 src[ssl[0]:ssl[1], ssl[2]:ssl[3]])
        mk.op(eng, fn, reads=[(src.name, *ssl)], writes=[(dst.name, *dsl)])

    def mm(out, osl, lhs, lsl, rhs, rsl, start, stop):
        def fn(h):
            return h.matmul(out[osl[0]:osl[1], osl[2]:osl[3]],
                            lhs[lsl[0]:lsl[1], lsl[2]:lsl[3]],
                            rhs[rsl[0]:rsl[1], rsl[2]:rsl[3]], start=start, stop=stop)
        mk.op("pe", fn,
              reads=[(lhs.name, *lsl), (rhs.name, *rsl)] +
                    ([(out.name, *osl)] if not start else []),
              writes=[(out.name, *osl)])

    ln_ctr = [0]

    def ln_tile(src, ssl, dst, dsl, rows=128, p0=0):
        b = (ln_ctr[0] % 8) * 6
        ln_ctr[0] += 1
        c_sum, c_negm, c_ssq, c_std, c_rstd = b, b + 1, b + 2, b + 3, b + 4
        R = rows
        mk.op("dve", lambda h: h.reduce_sum(lns[p0:p0 + R, c_sum:c_sum + 1], src[ssl[0]:ssl[1], ssl[2]:ssl[3]], axis=AX.X),
              reads=[(src.name, *ssl)], writes=[("lns", 0, 128, c_sum, c_sum + 1)])
        mk.op("dve", lambda h: h.tensor_scalar(lns[p0:p0 + R, c_negm:c_negm + 1], lns[p0:p0 + R, c_sum:c_sum + 1], -1.0 / D, None, ALU.mult),
              reads=[("lns", 0, 128, c_sum, c_sum + 1)], writes=[("lns", 0, 128, c_negm, c_negm + 1)])
        mk.op("act", lambda h: h.activation(dst[dsl[0]:dsl[1], dsl[2]:dsl[3]], src[ssl[0]:ssl[1], ssl[2]:ssl[3]],
                                            AF.Square, bias=lns[p0:p0 + R, c_negm:c_negm + 1], scale=f32(1.0),
                                            accum_out=lns[p0:p0 + R, c_ssq:c_ssq + 1]),
              reads=[(src.name, *ssl), ("lns", 0, 128, c_negm, c_negm + 1)],
              writes=[(dst.name, *dsl), ("lns", 0, 128, c_ssq, c_ssq + 1)])
        mk.op("act", lambda h: h.activation(lns[p0:p0 + R, c_std:c_std + 1], lns[p0:p0 + R, c_ssq:c_ssq + 1], AF.Ln,
                                            bias=f32(LNEPS), scale=f32(1.0 / D)),
              reads=[("lns", 0, 128, c_ssq, c_ssq + 1)], writes=[("lns", 0, 128, c_std, c_std + 1)])
        mk.op("act", lambda h: h.activation(lns[p0:p0 + R, c_rstd:c_rstd + 1], lns[p0:p0 + R, c_std:c_std + 1], AF.Exp,
                                            bias=f32(0.0), scale=f32(-0.5)),
              reads=[("lns", 0, 128, c_std, c_std + 1)], writes=[("lns", 0, 128, c_rstd, c_rstd + 1)])
        mk.op("dve", lambda h: h.tensor_scalar(dst[dsl[0]:dsl[1], dsl[2]:dsl[3]],
                                               src[ssl[0]:ssl[1], ssl[2]:ssl[3]],
                                               lns[p0:p0 + R, c_negm:c_negm + 1], lns[p0:p0 + R, c_rstd:c_rstd + 1], ALU.add, ALU.mult),
              reads=[(src.name, *ssl), ("lns", 0, 128, c_negm, c_negm + 1), ("lns", 0, 128, c_rstd, c_rstd + 1)],
              writes=[(dst.name, *dsl)])

    dma_in(kb_s, (0, 128, 0, 1), dkb, (0, 128, 0, 1), "dkb")
    dma_in(bqf, (0, 128, 0, ND), dbq, (0, 128, 0, ND), "dbq")
    dma_in(bkf, (0, 128, 0, ND), dbk, (0, 128, 0, ND), "dbk")
    mk.op("dve", lambda h: h.memset(ones_b[0:1, 0:128], 1.0), writes=[("ones_b", 0, 1, 0, 128)])

    st_phi = ExitStack()
    phi = sb("phi", [128, ND * 2048], BF16, st_phi, "left")

    st_A1 = ExitStack()
    xa = sb("xa", [128, 2 * 1024], F32, st_A1)
    ctxu = sb("ctxu", [128, 2 * 512], BF16, st_A1)
    ctxw = sb("ctxw", [64, 2 * 512], BF16, st_A1)
    wq = sb("wq", [128, ND * 1024], BF16, st_A1)
    wk = sb("wk", [128, ND * 1024], BF16, st_A1)
    wv = sb("wv", [128, ND * 1024], BF16, st_A1)
    xn_t = sb("xn_t", [128, 2 * 1024], BF16, st_A1)
    xn_fm = sb("xn_fm", [128, 2 * 4096], BF16, st_A1)
    kf = sb("kf", [128, 3 * 4096], BF16, st_A1)
    vt = sb("vt", [128, 3 * 4160], BF16, st_A1)
    VT_W = 3 * 4160

    cdma(ident_b, (0, 128, 0, 128), dident, (0, 128, 0, 128), "dident")
    dma_in(ident_f, (0, 128, 0, 128), dident, (0, 128, 0, 128), "dident")
    for (dsrc, key, off, n) in ((dbv, "dbv", 0, D), (dbo, "dbo", D, D)):
        cdma(rows_b, (0, 1, off, off + n), dsrc, (0, 1, 0, n), key)
    for w_s, w_d, key in ((wq, dwq, "dwq"), (wk, dwk, "dwk"), (wv, dwv, "dwv")):
        def fwl(h, w_s=w_s, w_d=w_d):
            return h.dma_start(
                out=bass.AP(w_s, 0, [[ND * 1024, 128], [1024, 8], [1, 1024]]),
                in_=bass.AP(w_d, 0, [[1024, 128], [1024 * 128, 8], [1, 1024]]))
        mk.dma("pool", fwl, reads=[(key, 0, D, 0, D)], writes=[(w_s.name, 0, 128, 0, ND * 1024)])

    def fphi(h):
        return h.dma_start(
            out=bass.AP(phi, 0, [[ND * 2048, 128], [2048, 8], [1, 2048]]),
            in_=bass.AP(dphi, 0, [[2048, 128], [2048 * 128, 8], [1, 2048]]))
    mk.dma("pool", fphi, reads=[("dphi", 0, D, 0, NP)], writes=[("phi", 0, 128, 0, ND * 2048)])
    qf = sb("qf", [128, 2 * 4096], BF16, st_A1)
    pbuf = sb("pbuf", [128, 2 * 4096], BF16, st_A1)

    for slot in range(3):
        for u in range(4):
            base = slot * 4160 + u * 1040

            def fo(h, base=base):
                return h.memset(bass.AP(vt, base + 64, [[VT_W, 128], [65, 16], [1, 1]]), 1.0)
            mk.op("dve", fo, writes=[("vt", 0, 128, base, base + 1040)])

    def xn_load(w, u):
        src_t, skey, r0 = (dxh, "dxh", 0) if w < 0 else (dx, "dx", 512 * w)
        asl = (u % 2) * 1024
        dma_in(xa, (0, 128, asl, asl + 1024), src_t, (r0 + u * 128, r0 + u * 128 + 128, 0, D), skey)

    def xn_ln(w, xslot, u):
        asl = (u % 2) * 1024
        tsl = (u % 2) * 1024
        ln_tile(xa, (0, 128, asl, asl + 1024), xn_t, (0, 128, tsl, tsl + 1024))

    def xn_tr(w, xslot, u):
        tsl = (u % 2) * 1024
        for dd in range(ND):
            def ftr(h, tsl=tsl, dd=dd):
                return h.transpose(ps_tr[0:128, dd * 128:dd * 128 + 128],
                                   xn_t[0:128, tsl + dd * 128:tsl + dd * 128 + 128],
                                   ident_b[0:128, 0:128])
            mk.op("pe", ftr, reads=[("xn_t", 0, 128, tsl + dd * 128, tsl + dd * 128 + 128),
                                    ("ident_b", 0, 128, 0, 128)],
                  writes=[("ps_tr", 0, 128, dd * 128, dd * 128 + 128)])
        XN_W = 2 * 4096

        def fev(h, u=u, xslot=xslot):
            return h.tensor_copy(bass.AP(xn_fm, xslot * 4096 + u * 128, [[XN_W, 128], [512, 8], [1, 128]]),
                                 ps_tr[0:128, 0:1024])
        mk.op("dve", fev, reads=[("ps_tr", 0, 128, 0, 1024)],
              writes=[("xn_fm", 0, 128, xslot * 4096, xslot * 4096 + 4096)])

    def projfm_unit(wt, bias_fm, xslot, dst, dst_off, jc):
        osl = (0, 128, 0, 512)
        for dd in range(ND):
            mm(ps_aux, osl, wt, (0, 128, dd * 1024 + jc * 128, dd * 1024 + jc * 128 + 128),
               xn_fm, (0, 128, xslot * 4096 + dd * 512, xslot * 4096 + dd * 512 + 512),
               start=(dd == 0), stop=(dd == ND - 1))
        dsl = (0, 128, dst_off + jc * 512, dst_off + jc * 512 + 512)

        def fb(h, osl=osl, dsl=dsl, jc=jc, bias_fm=bias_fm, dst=dst):
            return h.tensor_scalar(dst[dsl[0]:dsl[1], dsl[2]:dsl[3]],
                                   ps_aux[osl[0]:osl[1], osl[2]:osl[3]],
                                   bias_fm[0:128, jc:jc + 1], None, ALU.add)
        mk.op("dve", fb, reads=[("ps_aux", *osl), (bias_fm.name, 0, 128, jc, jc + 1)],
              writes=[(dst.name, *dsl)])

    def v_unit(xslot, vslot, u, jh):
        osl = (0, 128, 0, 512)
        for dd in range(ND):
            mm(ps_aux, osl,
               xn_fm, (0, 128, xslot * 4096 + dd * 512 + u * 128,
                       xslot * 4096 + dd * 512 + u * 128 + 128),
               wv, (0, 128, dd * 1024 + jh * 512, dd * 1024 + jh * 512 + 512),
               start=(dd == 0), stop=False)
        mm(ps_aux, osl, ones_b, (0, 1, 0, 128), rows_b, (0, 1, jh * 512, jh * 512 + 512),
           start=False, stop=True)
        base = vslot * 4160 + u * 1040 + 8 * jh * 65

        def fe(h, osl=osl, base=base):
            return h.tensor_copy(bass.AP(vt, base, [[VT_W, 128], [65, 8], [1, 64]]),
                                 ps_aux[osl[0]:osl[1], osl[2]:osl[3]])
        mk.op("dve", fe, reads=[("ps_aux", *osl)],
              writes=[("vt", 0, 128, base, base + 8 * 65)])

    def sgS(c, slot_prev, slot_cur, hp, s):
        heads = (2 * hp, 2 * hp + 1)
        jc = hp
        for h_ in heads:
            hb = 64 * (h_ % 2)
            qoff = (c % 2) * 4096 + jc * 512
            bb = (h_ % 2) * 1024
            for j in range(2):
                kc = 2 * s + j
                slot = slot_prev if kc < 4 else slot_cur
                kk = kc % 4
                mm(ps_big, (0, 128, bb + j * 512, bb + j * 512 + 512),
                   kf, (hb, hb + 64, slot * 4096 + jc * 512 + kk * 128,
                        slot * 4096 + jc * 512 + kk * 128 + 128),
                   qf, (hb, hb + 64, qoff, qoff + 512),
                   start=True, stop=True)
        halo = (c == 0 and s < 2)

        def fx(h, s=s, halo=halo):
            bias = kb_s[0:128, 0:1] if halo else f32(-C1)
            return h.activation(pbuf[0:128, s * 2048:s * 2048 + 2048],
                                ps_big[0:128, 0:2048], AF.Exp, bias=bias, scale=f32(0.125))
        mk.op("act", fx, reads=[("ps_big", 0, 128, 0, 2048)] +
              ([("kb_s", 0, 128, 0, 1)] if halo else []),
              writes=[("pbuf", 0, 128, s * 2048, s * 2048 + 2048)])

    def sgA(c, slot_prev, slot_cur, hp, s):
        for h_ in (2 * hp, 2 * hp + 1):
            for kc in (2 * s, 2 * s + 1):
                slot = slot_prev if kc < 4 else slot_cur
                u = kc % 4
                po = (kc // 2) * 2048 + (h_ % 2) * 1024 + (kc % 2) * 512
                mm(ps_cx, (0, 65, (h_ % 2) * 512, (h_ % 2) * 512 + 512),
                   vt, (0, 128, slot * 4160 + u * 1040 + 65 * h_,
                        slot * 4160 + u * 1040 + 65 * h_ + 65),
                   pbuf, (0, 128, po, po + 512),
                   start=(kc == 0), stop=(kc == 7))

    def attn_norm_a(c, hp):
        for h_ in (2 * hp, 2 * hp + 1):
            hsl = (h_ % 2) * 512

            def fev(h, hsl=hsl):
                return h.tensor_copy(ctxu[0:64, hsl:hsl + 512], ps_cx[0:64, hsl:hsl + 512])
            mk.op("dve", fev, reads=[("ps_cx", 0, 64, hsl, hsl + 512)],
                  writes=[("ctxu", 0, 64, hsl, hsl + 512)])
        mk.op("act", lambda h: h.activation(lnr[0:1, 0:1024], ps_cx[64:65, 0:1024], AF.Ln,
                                            bias=f32(0.0), scale=f32(1.0)),
              reads=[("ps_cx", 64, 65, 0, 1024)], writes=[("lnr", 0, 1, 0, 1024)])
        mk.op("act", lambda h: h.activation(rrow[0:1, 0:1024], lnr[0:1, 0:1024], AF.Exp,
                                            bias=f32(0.0), scale=f32(-1.0)),
              reads=[("lnr", 0, 1, 0, 1024)], writes=[("rrow", 0, 1, 0, 1024)])

    def attn_norm_b(c, hp):
        jc = hp
        for h_ in (2 * hp, 2 * hp + 1):
            hsl = (h_ % 2) * 512
            mm(ps_aux, (0, 64, 0, 512), ones_b, (0, 1, 0, 64), rrow, (0, 1, hsl, hsl + 512),
               start=True, stop=True)

            def fm0(h, hsl=hsl):
                return h.tensor_tensor(ctxw[0:64, hsl:hsl + 512], ctxu[0:64, hsl:hsl + 512],
                                       ps_aux[0:64, 0:512], ALU.mult)
            mk.op("dve", fm0, reads=[("ctxu", 0, 64, hsl, hsl + 512), ("ps_aux", 0, 64, 0, 512)],
                  writes=[("ctxw", 0, 64, hsl, hsl + 512)])
            rb = jc * 128 + (h_ % 2) * 64

            def fd(h, rb=rb, c=c, hsl=hsl):
                return h.dma_start(
                    out=bass.AP(ctx_d, rb * TOK + c * 512, [[TOK, 64], [1, 512]]),
                    in_=ctxw[0:64, hsl:hsl + 512])
            mk.dma("sp", fd, reads=[("ctxw", 0, 64, hsl, hsl + 512)],
                   writes=[("ctx_d", rb, rb + 64, c * 512, c * 512 + 512)])

    def prologue_thunks(w):
        th = []
        xslot = (w + 1) % 2
        kvslot = (w + 1) % 3
        lds_th = [lambda u=u, w=w: xn_load(w, u) for u in range(4)]
        lns_th = [lambda u=u, w=w, xslot=xslot: xn_ln(w, xslot, u) for u in range(4)]
        trs_th = [lambda u=u, w=w, xslot=xslot: xn_tr(w, xslot, u) for u in range(4)]
        th.append(lds_th[0]); th.append(lds_th[1]); th.append(lns_th[0])
        th.append(lds_th[2]); th.append(lns_th[1]); th.append(trs_th[0])
        th.append(lds_th[3]); th.append(lns_th[2]); th.append(trs_th[1])
        th.append(lns_th[3]); th.append(trs_th[2]); th.append(trs_th[3])
        for jc in range(ND):
            th.append(lambda jc=jc, xslot=xslot, kvslot=kvslot:
                      projfm_unit(wk, bkf, xslot, kf, kvslot * 4096, jc))
        for u in range(4):
            for jh in range(2):
                th.append(lambda u=u, jh=jh, xslot=xslot, kvslot=kvslot:
                          v_unit(xslot, kvslot, u, jh))
        if w >= 0:
            for jc in range(ND):
                th.append(lambda jc=jc, xslot=xslot, w=w:
                          projfm_unit(wq, bqf, xslot, qf, (w % 2) * 4096, jc))
        return th

    for t_ in prologue_thunks(-1):
        t_()
    for t_ in prologue_thunks(0):
        t_()
    tap("xnfm", xn_fm, 0, 128, 4096, 8192)
    tap("kf", kf, 0, 128, 4096, 8192)
    tap("vt", vt, 0, 128, 4160, 8320)
    tap("qf", qf, 0, 128, 0, 4096)
    NPAIR = H // 2
    pairs = [(w, hp) for w in range(NCH) for hp in range(NPAIR)]
    thunk_map = {}
    for w in range(NCH - 1):
        th = prologue_thunks(w + 1)
        per = (len(th) + NPAIR - 2) // (NPAIR - 1)
        for j in range(NPAIR - 1):
            thunk_map[(w, j)] = th[j * per:(j + 1) * per]
    tap_pb_after = (0, NPAIR - 1)

    prevarg = None
    for i in range(len(pairs)):
        w, hp = pairs[i]
        arg = (w, w % 3, (w + 1) % 3, hp)
        sgS(*arg, 0)
        if prevarg is not None:
            sgA(*prevarg, 3)
            attn_norm_a(prevarg[0], prevarg[3])
        sgS(*arg, 1)
        sgA(*arg, 0)
        sgS(*arg, 2)
        if prevarg is not None:
            attn_norm_b(prevarg[0], prevarg[3])
        sgA(*arg, 1)
        sgS(*arg, 3)
        sgA(*arg, 2)
        for t_ in thunk_map.get(pairs[i], ()):
            t_()
        if pairs[i] == (0, NPAIR - 1):
            tap("pb", pbuf, 0, 128, 4096, 8192)
        prevarg = arg
    sgA(*prevarg, 3)
    attn_norm_a(prevarg[0], prevarg[3])
    attn_norm_b(prevarg[0], prevarg[3])

    mk.barrier()
    st_A1.close()

    phibrow = sb("phibrow", [1, 2048], BF16, st_phi, "left")
    rP = sb("rP", [128, NT * 64], F32, st_outer)
    rN = sb("rN", [128, NT * 32], F32, st_outer)
    E = sb("E", [128, NT * 2048], BF16, st_outer)
    st_AB = ExitStack()
    xn2t = sb("xn2t", [128, NT * 1024], BF16, st_AB)
    st_A2 = ExitStack()
    wo = sb("wo", [128, ND * 1024], BF16, st_A2)
    x2b = sb("x2b", [128, 2 * 1024], F32, st_A2)
    xn2fm = sb("xn2fm", [128, 2 * 1024], BF16, st_A2)
    ctxr = sb("ctxr", [128, 2 * 1024], BF16, st_A2)
    xa2 = sb("xa2", [128, 2 * 1024], F32, st_A2)
    E_W = NT * 2048
    CR_W = 2 * 1024

    cdma(phibrow, (0, 1, 0, NP), dphib, (0, 1, 0, NP), "dphib")
    for dd in range(ND):
        cdma(wo, (0, 128, dd * 1024, dd * 1024 + 1024), dwo,
             (dd * 128, dd * 128 + 128, 0, D), "dwo")

    def ctxr_load(t):
        c, u = t // 4, t % 4
        slot = (t % 2) * 1024

        def fl(h, c=c, u=u, slot=slot):
            return h.dma_start(
                out=bass.AP(ctxr, slot, [[CR_W, 128], [128, 8], [1, 128]]),
                in_=bass.AP(ctx_d, c * 512 + u * 128,
                            [[TOK, 128], [128 * TOK, 8], [1, 128]]))
        mk.dma("sp", fl, reads=[("ctx_d", 0, D, c * 512 + u * 128, c * 512 + u * 128 + 128)],
               writes=[("ctxr", 0, 128, slot, slot + 1024)])

    def xa2_load(t):
        asl = (t % 2) * 1024
        dma_in(xa2, (0, 128, asl, asl + 1024), dx, (t * 128, t * 128 + 128, 0, D), "dx", q="pool")

    def a2_wo(t):
        asl = (t % 2) * 1024
        bsl = (t % 2) * 1024
        slot = (t % 2) * 1024
        for jh in range(2):
            osl = (0, 128, jh * 512, jh * 512 + 512)
            for dd in range(ND):
                mm(ps_cx, osl, ctxr, (0, 128, slot + dd * 128, slot + dd * 128 + 128),
                   wo, (0, 128, dd * 1024 + jh * 512, dd * 1024 + jh * 512 + 512),
                   start=(dd == 0), stop=False)
            mm(ps_cx, osl, ones_b, (0, 1, 0, 128), rows_b, (0, 1, D + jh * 512, D + jh * 512 + 512),
               start=False, stop=True)

            def fa(h, jh=jh, asl=asl, bsl=bsl, osl=osl):
                return h.tensor_tensor(x2b[0:128, bsl + jh * 512:bsl + jh * 512 + 512],
                                       xa2[0:128, asl + jh * 512:asl + jh * 512 + 512],
                                       ps_cx[osl[0]:osl[1], osl[2]:osl[3]], ALU.add)
            mk.op("dve", fa, reads=[("xa2", 0, 128, asl + jh * 512, asl + jh * 512 + 512),
                                    ("ps_cx", *osl)],
                  writes=[("x2b", 0, 128, bsl + jh * 512, bsl + jh * 512 + 512)])

        def fw(h, t=t, bsl=bsl):
            return h.dma_start(out=x2_d[t * 128:t * 128 + 128, 0:D], in_=x2b[0:128, bsl:bsl + 1024])
        mk.dma("pool", fw, reads=[("x2b", 0, 128, bsl, bsl + 1024)],
               writes=[("x2_d", t * 128, t * 128 + 128, 0, D)])

    def ab_ln(t):
        bsl = (t % 2) * 1024
        ln_tile(x2b, (0, 128, bsl, bsl + 1024), xn2t, (0, 128, t * 1024, t * 1024 + 1024))

    def ab_tr(t):
        fsl = (t % 2) * 1024
        for dd in range(ND):
            def ftr(h, t=t, dd=dd):
                return h.transpose(ps_tr[0:128, dd * 128:dd * 128 + 128],
                                   xn2t[0:128, t * 1024 + dd * 128:t * 1024 + dd * 128 + 128],
                                   ident_b[0:128, 0:128])
            mk.op("pe", ftr, reads=[("xn2t", 0, 128, t * 1024 + dd * 128, t * 1024 + dd * 128 + 128),
                                    ("ident_b", 0, 128, 0, 128)],
                  writes=[("ps_tr", 0, 128, dd * 128, dd * 128 + 128)])

        def fcv(h, fsl=fsl):
            return h.copy(xn2fm[0:128, fsl:fsl + 1024], ps_tr[0:128, 0:1024])
        mk.op("act", fcv, reads=[("ps_tr", 0, 128, 0, 1024)],
              writes=[("xn2fm", 0, 128, fsl, fsl + 1024)])

    B1SLOTS = ((ps_big, "ps_big", 0), (ps_big, "ps_big", 1024))

    def b_logits(t):
        fsl = (t % 2) * 1024
        for half in range(2):
            pst, pkey, po = B1SLOTS[half]
            for es in (2 * half, 2 * half + 1):
                osl = (0, 128, po + (es % 2) * 512, po + (es % 2) * 512 + 512)
                for dd in range(ND):
                    mm(pst, osl, xn2fm, (0, 128, fsl + dd * 128, fsl + dd * 128 + 128),
                       phi, (0, 128, dd * 2048 + es * 512, dd * 2048 + es * 512 + 512),
                       start=(dd == 0), stop=False)
                mm(pst, osl, ones_b, (0, 1, 0, 128),
                   phibrow, (0, 1, es * 512, es * 512 + 512), start=False, stop=True)

        def fx(h, t=t):
            return h.activation(E[0:128, t * 2048:t * 2048 + 2048],
                                ps_big[0:128, 0:2048],
                                AF.Exp, bias=f32(-C2), scale=f32(1.0))
        mk.op("act", fx, reads=[("ps_big", 0, 128, 0, 2048)],
              writes=[("E", 0, 128, t * 2048, t * 2048 + 2048)])

    def b_reduce(t):
        def fp(h, t=t):
            return h.reduce_sum(rP[0:128, t * 64:t * 64 + 64],
                                bass.AP(E, t * 2048, [[E_W, 128], [32, 64], [1, 32]]), axis=AX.X)
        mk.op("dve", fp, reads=[("E", 0, 128, t * 2048, t * 2048 + 2048)],
              writes=[("rP", 0, 128, t * 64, t * 64 + 64)])
        mk.op("dve", lambda h, t=t: h.reciprocal(rP[0:128, t * 64:t * 64 + 64], rP[0:128, t * 64:t * 64 + 64]),
              reads=[("rP", 0, 128, t * 64, t * 64 + 64)], writes=[("rP", 0, 128, t * 64, t * 64 + 64)])

        def fn_(h, t=t):
            return h.reduce_sum(rN[0:128, t * 32:t * 32 + 32],
                                bass.AP(E, t * 2048, [[E_W, 128], [1, 32], [32, 64]]), axis=AX.X)
        mk.op("dve", fn_, reads=[("E", 0, 128, t * 2048, t * 2048 + 2048)],
              writes=[("rN", 0, 128, t * 32, t * 32 + 32)])
        mk.op("dve", lambda h, t=t: h.reciprocal(rN[0:128, t * 32:t * 32 + 32], rN[0:128, t * 32:t * 32 + 32]),
              reads=[("rN", 0, 128, t * 32, t * 32 + 32)], writes=[("rN", 0, 128, t * 32, t * 32 + 32)])

    ctxr_load(0)
    xa2_load(0)
    for i in range(NT + 2):
        if i < NT:
            if i + 1 < NT:
                ctxr_load(i + 1)
                xa2_load(i + 1)
            a2_wo(i)
            ab_ln(i)
        if 1 <= i <= NT:
            ab_tr(i - 1)
        if i >= 2:
            b_logits(i - 2)
            b_reduce(i - 2)
    tap("E", E, 0, 128, 0, 2048)
    tap("rP", rP, 0, 128, 0, NT * 64)
    tap("rN", rN, 0, 128, 0, NT * 32)
    if "x2" in taps:
        for t in range(NT):
            def fxt(h, t=t):
                return h.dma_start(out=tap_d["x2"][t * 128:t * 128 + 128, 0:D],
                                   in_=x2_d[t * 128:t * 128 + 128, 0:D])
            mk.dma("pool", fxt, reads=[("x2_d", t * 128, t * 128 + 128, 0, D)],
                   writes=[("tap_x2", t * 128, t * 128 + 128, 0, D)])

    mk.barrier()
    st_A2.close()
    st_phi.close()

    st_Y = ExitStack()
    ytlE = sb("ytlE", [128, (NPC // 2) * 1024], BF16, st_Y)
    st_BC = ExitStack()
    dslab = sb("dslab", [128, 2 * NT * 128], BF16, st_BC)
    rpx = sb("rpx", [128, 2 * 128], BF16, st_BC)
    xps = sb("xps", [128, 2 * 1024], F32, st_BC)
    xtb = sb("xtb", [128, 2 * 1024], F32, st_BC)
    xtn = sb("xtn", [128, 2 * 1024], BF16, st_BC)
    xtT = sb("xtT", [128, ND * 256], BF16, st_BC)
    EWR = 16
    ewb = sb("ewb", [128, EWR * 1024], BF16, st_BC)
    ebrow = sb("ebrow", [1, 4 * 1024], BF16, st_BC)
    ytlo = sb("ytlo", [32, 2 * 1024], BF16, st_BC)

    def ew_load(k):
        esl = (k % EWR) * 1024
        cdma(ewb, (0, 128, esl, esl + 1024), dew, (k * 128, k * 128 + 128, 0, D), "dew")

    def xpart_npc(npc):
        dsl0 = (npc % 2) * (NT * 128)

        def fdm(h, npc=npc, dsl0=dsl0):
            return h.tensor_tensor(
                bass.AP(dslab, dsl0, [[2 * NT * 128, 128], [128, 16], [1, 128]]),
                bass.AP(E, npc * 128, [[E_W, 128], [2048, 16], [1, 128]]),
                bass.AP(rP, npc * 4, [[NT * 64, 128], [64, 16], [1, 4], [0, 32]]),
                ALU.mult)
        mk.op("dve", fdm, reads=[("E", 0, 128, 0, E_W), ("rP", 0, 128, 0, NT * 64)],
              writes=[("dslab", 0, 128, dsl0, dsl0 + NT * 128)])
        pst, po = (ps_cx, 0) if (npc // 2) % 2 == 0 else (ps_big, 1024)
        for dh_ in range(2):
            osl = (0, 128, po + dh_ * 512, po + dh_ * 512 + 512)
            for t in range(NT):
                mm(pst, osl, dslab, (0, 128, dsl0 + t * 128, dsl0 + t * 128 + 128),
                   xn2t, (0, 128, t * 1024 + dh_ * 512, t * 1024 + dh_ * 512 + 512),
                   start=(t == 0), stop=(t == NT - 1))
            cast(xps, (0, 128, (npc % 2) * 1024 + dh_ * 512, (npc % 2) * 1024 + dh_ * 512 + 512),
                 pst, (0, 128, osl[2], osl[3]), eng="act")
        for hq in range(2):
            r0 = 1024 * (npc % 2) + 512 * hq + 64 * (npc // 2)

            def fxp(h, npc=npc, r0=r0, hq=hq):
                return h.dma_start(out=xp_d[r0:r0 + 64, 0:D],
                                   in_=xps[64 * hq:64 * hq + 64,
                                           (npc % 2) * 1024:(npc % 2) * 1024 + 1024])
            mk.dma("pool", fxp, reads=[("xps", 64 * hq, 64 * hq + 64,
                                        (npc % 2) * 1024, (npc % 2) * 1024 + 1024)],
                   writes=[("xp_d", r0, r0 + 64, 0, D)])

    def rs_half(G, hq):
        r0 = 1024 * G + 512 * hq
        o0 = 128 * G + 64 * hq

        def frs(h, r0=r0, o0=o0):
            return h.collective_compute("ReduceScatter", ALU.add, replica_groups=RG,
                                        ins=[xp_d[r0:r0 + 512, 0:D].opt()],
                                        outs=[xt_d[o0:o0 + 64, 0:D].opt()])
        mk.collective(frs, reads=[("xp_d", r0, r0 + 512, 0, D)],
                      writes=[("xt_d", o0, o0 + 64, 0, D)])

    def rs_slab(G):
        rs_half(G, 0)
        rs_half(G, 1)

    def xtproc(G):
        xsl = G * 1024

        def fxt(h, G=G, xsl=xsl):
            return h.dma_start(out=xtb[0:128, xsl:xsl + 1024], in_=xt_d[128 * G:128 * G + 128, 0:D])
        mk.dma("sp", fxt, reads=[("xt_d", 128 * G, 128 * G + 128, 0, D)],
               writes=[("xtb", 0, 128, xsl, xsl + 1024)])

        def fcl(h, xsl=xsl):
            return h.tensor_scalar(xtb[0:128, xsl:xsl + 1024], xtb[0:128, xsl:xsl + 1024],
                                   -33000.0, 65000.0, ALU.max, ALU.min)
        mk.op("dve", fcl, reads=[("xtb", 0, 128, xsl, xsl + 1024)],
              writes=[("xtb", 0, 128, xsl, xsl + 1024)])
        ln_tile(xtb, (0, 128, xsl, xsl + 1024), xtn, (0, 128, xsl, xsl + 1024))

        def ftn(h, xsl=xsl):
            return h.activation(xtn[0:128, xsl:xsl + 1024], xtn[0:128, xsl:xsl + 1024],
                                AF.Tanh, bias=f32(0.0), scale=f32(1.0))
        mk.op("act", ftn, reads=[("xtn", 0, 128, xsl, xsl + 1024)],
              writes=[("xtn", 0, 128, xsl, xsl + 1024)])
        for dd in range(ND):
            def ftr(h, dd=dd, xsl=xsl):
                return h.transpose(ps_tr[0:128, dd * 128:dd * 128 + 128],
                                   xtn[0:128, xsl + dd * 128:xsl + dd * 128 + 128],
                                   ident_b[0:128, 0:128])
            mk.op("pe", ftr, reads=[("xtn", 0, 128, xsl + dd * 128, xsl + dd * 128 + 128),
                                    ("ident_b", 0, 128, 0, 128)],
                  writes=[("ps_tr", 0, 128, dd * 128, dd * 128 + 128)])

        def fevT(h, G=G):
            return h.tensor_copy(bass.AP(xtT, 128 * G, [[ND * 256, 128], [256, 8], [1, 128]]),
                                 ps_tr[0:128, 0:1024])
        mk.op("dve", fevT, reads=[("ps_tr", 0, 128, 0, 1024)],
              writes=[("xtT", 0, 128, 0, ND * 256)])
        if "xtn" in taps:
            tap("xtn", xtn, 0, 128, xsl, xsl + 1024, dst_r0=128 * G)

    def xtproc_half(G, hq):
        xsl = G * 1024
        pb = 64 * hq

        def fxt(h, G=G, xsl=xsl, pb=pb):
            return h.dma_start(out=xtb[pb:pb + 64, xsl:xsl + 1024],
                               in_=xt_d[128 * G + pb:128 * G + pb + 64, 0:D])
        mk.dma("sp", fxt, reads=[("xt_d", 128 * G + pb, 128 * G + pb + 64, 0, D)],
               writes=[("xtb", pb, pb + 64, xsl, xsl + 1024)])

        def fcl(h, xsl=xsl, pb=pb):
            return h.tensor_scalar(xtb[pb:pb + 64, xsl:xsl + 1024], xtb[pb:pb + 64, xsl:xsl + 1024],
                                   -33000.0, 65000.0, ALU.max, ALU.min)
        mk.op("dve", fcl, reads=[("xtb", pb, pb + 64, xsl, xsl + 1024)],
              writes=[("xtb", pb, pb + 64, xsl, xsl + 1024)])
        ln_tile(xtb, (pb, pb + 64, xsl, xsl + 1024), xtn, (pb, pb + 64, xsl, xsl + 1024),
                rows=64, p0=pb)

        def ftn(h, xsl=xsl, pb=pb):
            return h.activation(xtn[pb:pb + 64, xsl:xsl + 1024], xtn[pb:pb + 64, xsl:xsl + 1024],
                                AF.Tanh, bias=f32(0.0), scale=f32(1.0))
        mk.op("act", ftn, reads=[("xtn", pb, pb + 64, xsl, xsl + 1024)],
              writes=[("xtn", pb, pb + 64, xsl, xsl + 1024)])
        for dd in range(ND):
            def ftr(h, dd=dd, xsl=xsl, pb=pb):
                return h.transpose(ps_tr[0:128, dd * 64:dd * 64 + 64],
                                   xtn[pb:pb + 64, xsl + dd * 128:xsl + dd * 128 + 128],
                                   ident_b[pb:pb + 64, pb:pb + 64])
            mk.op("pe", ftr, reads=[("xtn", pb, pb + 64, xsl + dd * 128, xsl + dd * 128 + 128),
                                    ("ident_b", pb, pb + 64, pb, pb + 64)],
                  writes=[("ps_tr", 0, 128, dd * 64, dd * 64 + 64)])

        def fevT(h, G=G, pb=pb):
            return h.tensor_copy(bass.AP(xtT, 128 * G + pb, [[ND * 256, 128], [256, 8], [1, 64]]),
                                 ps_tr[0:128, 0:512])
        mk.op("dve", fevT, reads=[("ps_tr", 0, 128, 0, 512)],
              writes=[("xtT", 0, 128, 0, ND * 256)])

    def eb_load(n):
        esl = (n % 4) * 1024
        cdma(ebrow, (0, 1, esl, esl + 1024), deb, (n, n + 1, 0, D), "deb")

    def expert_unit(n):
        esl = (n % 4) * 1024
        if n + 2 < EL:
            eb_load(n + 2)
        for dd in range(ND):
            k = n * ND + dd
            wsl = (k % EWR) * 1024
            for hh in range(2):
                pst, psl = (ps_aux, (0, 32, 0, 512)) if hh == 0 else (ps_big, (0, 32, 0, 512))
                mm(pst, psl, xtT, (0, 128, dd * 256 + 32 * n, dd * 256 + 32 * n + 32),
                   ewb, (0, 128, wsl + hh * 512, wsl + hh * 512 + 512),
                   start=(dd == 0), stop=False)
            if k + EWR < EL * ND:
                ew_load(k + EWR)
        for hh in range(2):
            pst = ps_aux if hh == 0 else ps_big
            psl = (0, 32, 0, 512)
            mm(pst, psl, ones_b, (0, 1, 0, 32), ebrow, (0, 1, esl + hh * 512, esl + hh * 512 + 512),
               start=False, stop=True)

            def fy(h, n=n, hh=hh, pst=pst):
                return h.copy(ytlo[0:32, (n % 2) * 1024 + hh * 512:(n % 2) * 1024 + hh * 512 + 512],
                              pst[0:32, 0:512])
            mk.op("act", fy, reads=[(pst.name, 0, 32, 0, 512)],
                  writes=[("ytlo", 0, 32, (n % 2) * 1024 + hh * 512, (n % 2) * 1024 + hh * 512 + 512)])

        def fg(h, n=n):
            return h.dma_start(out=yg_in[32 * n:32 * n + 32, 0:D],
                               in_=ytlo[0:32, (n % 2) * 1024:(n % 2) * 1024 + 1024])
        mk.dma("pool", fg, reads=[("ytlo", 0, 32, (n % 2) * 1024, (n % 2) * 1024 + 1024)],
               writes=[("yg_in", 32 * n, 32 * n + 32, 0, D)])

    ew_k = [0]

    def emit_ew(nmax):
        while ew_k[0] < min(nmax, EWR):
            ew_load(ew_k[0])
            ew_k[0] += 1

    evens = [npc for npc in range(NPC) if npc % 2 == 0]
    odds = [npc for npc in range(NPC) if npc % 2 == 1]
    for i, npc in enumerate(evens):
        xpart_npc(npc)
        emit_ew((i + 1) * 2)
    rs_slab(0)
    for i, npc in enumerate(odds):
        xpart_npc(npc)
        if i == 4:
            xtproc(0)
            eb_load(0)
            eb_load(1)
        if i in (5, 6, 7):
            expert_unit(i - 5)
    rs_slab(1)
    expert_unit(3)
    if "xp" in taps:
        for g in range(2):
            def ftp(h, g=g):
                return h.dma_start(out=tap_d["xp"][1024 * g:1024 * g + 1024, 0:D],
                                   in_=xp_d[1024 * g:1024 * g + 1024, 0:D])
            mk.dma("pool", ftp, reads=[("xp_d", 1024 * g, 1024 * g + 1024, 0, D)],
                   writes=[("tap_xp", 1024 * g, 1024 * g + 1024, 0, D)])

    def fag1(h):
        return h.collective_compute("AllGather", ALU.bypass, replica_groups=RG,
                                    ins=[yg_in[0:128, 0:D].opt()], outs=[yg_out[0:1024, 0:D].opt()])

    def fag2(h):
        return h.collective_compute("AllGather", ALU.bypass, replica_groups=RG,
                                    ins=[yg_in[128:256, 0:D].opt()], outs=[yg_out[1024:2048, 0:D].opt()])

    mk.collective(fag1, reads=[("yg_in", 0, 128, 0, D)], writes=[("yg_out", 0, 1024, 0, D)])
    xtproc_half(1, 0)
    for npc in range(0, NPC, 2):
        r = 128 * (npc // 2)
        dma_in(ytlE, (0, 128, r * 8, r * 8 + 1024), yg_out, (r, r + 128, 0, D), "yg_out",
               q=("sp" if (npc // 2) % 2 == 0 else "pool"))
    expert_unit(4)
    expert_unit(5)
    xtproc_half(1, 1)
    expert_unit(6)
    expert_unit(7)
    mk.collective(fag2, reads=[("yg_in", 128, 256, 0, D)], writes=[("yg_out", 1024, 2048, 0, D)])

    mk.barrier(exclude=("cc",))
    st_BC.close()

    st_C = ExitStack()
    ytlO = sb("ytlO", [128, (NPC // 2) * 1024], BF16, st_C)
    rnx = sb("rnx", [128, 3 * 128], BF16, st_C)
    cbuf = sb("cbuf", [128, 2 * 128], F32, st_C)
    ctb = sb("ctb", [128, 3 * 2048], BF16, st_C)
    obuf = sb("obuf", [128, 2 * 1024], F32, st_C)
    x2c = sb("x2c", [128, 4 * 1024], F32, st_C)

    def ytl_row(npc):
        c_ = npc // 2
        return (1024 if npc % 2 else 0) + 128 * c_

    YORD = [npc for npc in range(NPC) if npc % 2 == 0] + [npc for npc in range(NPC) if npc % 2 == 1]
    for npc in range(1, NPC, 2):
        r = ytl_row(npc)
        dma_in(ytlO, (0, 128, (npc // 2) * 1024, (npc // 2) * 1024 + 1024),
               yg_out, (r, r + 128, 0, D), "yg_out",
               q=("sp" if (npc // 2) % 2 == 0 else "pool"))

    def ytl_sl(npc):
        return (ytlE, (npc // 2) * 1024) if npc % 2 == 0 else (ytlO, (npc // 2) * 1024)
    if "ytl" in taps:
        for npc in range(NPC):
            yb, yo = ytl_sl(npc)
            tap("ytl", yb, 0, 128, yo, yo + 1024, dst_r0=npc * 128)

    for tp_ in range(2):
        dma_in(x2c, (0, 128, tp_ * 1024, tp_ * 1024 + 1024), x2_d,
               (tp_ * 128, tp_ * 128 + 128, 0, D), "x2_d")

    def y_rnx(t):
        rsl = (t % 3) * 128

        def frn(h, t=t, rsl=rsl):
            return h.tensor_copy(bass.AP(rnx, rsl, [[3 * 128, 128], [32, 4], [1, 32]]),
                                 bass.AP(rN, t * 32, [[NT * 32, 128], [0, 4], [1, 32]]))
        mk.op("dve", frn, reads=[("rN", 0, 128, t * 32, t * 32 + 32)],
              writes=[("rnx", 0, 128, rsl, rsl + 128)])

    def y_prep_i(t, i):
        rsl = (t % 3) * 128
        npc = YORD[i]
        csl = (i % 2) * 128
        pst, pkey = (ps_aux, "ps_aux") if i % 2 == 0 else (ps_big, "ps_big")

        def fc(h, t=t, npc=npc, csl=csl, rsl=rsl):
            return h.tensor_tensor(cbuf[0:128, csl:csl + 128],
                                   E[0:128, t * 2048 + npc * 128:t * 2048 + npc * 128 + 128],
                                   rnx[0:128, rsl:rsl + 128], ALU.mult)
        mk.op("dve", fc, reads=[("E", 0, 128, t * 2048 + npc * 128, t * 2048 + npc * 128 + 128),
                                ("rnx", 0, 128, rsl, rsl + 128)],
              writes=[("cbuf", 0, 128, csl, csl + 128)])

        def ft(h, csl=csl, pst=pst):
            return h.transpose(pst[0:128, 0:128], cbuf[0:128, csl:csl + 128],
                               ident_f[0:128, 0:128])
        mk.op("pe", ft, reads=[("cbuf", 0, 128, csl, csl + 128), ("ident_f", 0, 128, 0, 128)],
              writes=[(pkey, 0, 128, 0, 128)])
        dsl = (t % 3) * 2048 + i * 128

        def fe(h, dsl=dsl, pst=pst):
            return h.copy(ctb[0:128, dsl:dsl + 128], pst[0:128, 0:128])
        mk.op("act", fe, reads=[(pkey, 0, 128, 0, 128)],
              writes=[("ctb", 0, 128, dsl, dsl + 128)])

    def y_mm(t):
        xsl = (t % 4) * 1024
        if t + 2 < NT:
            dma_in(x2c, (0, 128, ((t + 2) % 4) * 1024, ((t + 2) % 4) * 1024 + 1024), x2_d,
                   ((t + 2) * 128, (t + 2) * 128 + 128, 0, D), "x2_d")
            y_rnx(t + 2)
        for i in range(NPC):
            npc = YORD[i]
            dsl = (t % 3) * 2048 + i * 128
            yb, yo = ytl_sl(npc)
            for hh in range(2):
                mm(ps_cx, (0, 128, hh * 512, hh * 512 + 512), ctb, (0, 128, dsl, dsl + 128),
                   yb, (0, 128, yo + hh * 512, yo + hh * 512 + 512),
                   start=(i == 0), stop=(i == NPC - 1))
            if t + 2 < NT:
                y_prep_i(t + 2, i)
        osl = (t % 2) * 1024
        for hh in range(2):
            def fo(h, hh=hh, osl=osl, xsl=xsl):
                return h.tensor_tensor(obuf[0:128, osl + hh * 512:osl + hh * 512 + 512],
                                       x2c[0:128, xsl + hh * 512:xsl + hh * 512 + 512],
                                       ps_cx[0:128, hh * 512:hh * 512 + 512], ALU.add)
            mk.op("dve", fo, reads=[("x2c", 0, 128, xsl + hh * 512, xsl + hh * 512 + 512),
                                    ("ps_cx", 0, 128, hh * 512, hh * 512 + 512)],
                  writes=[("obuf", 0, 128, osl + hh * 512, osl + hh * 512 + 512)])

        def fod(h, t=t, osl=osl):
            return h.dma_start(out=do[t * 128:t * 128 + 128, 0:D], in_=obuf[0:128, osl:osl + 1024])
        mk.dma("pool", fod, reads=[("obuf", 0, 128, osl, osl + 1024)],
               writes=[("do", t * 128, t * 128 + 128, 0, D)])

    for t_ in range(2):
        y_rnx(t_)
        for i_ in range(NPC):
            y_prep_i(t_, i_)
    for t in range(NT):
        y_mm(t)
    mk.wait_all("pool", DMA_POOLS["pool"] + DMA_POOLS["sp"] + ["cc"])

    st_C.close()
    st_Y.close()
    st_AB.close()
    st_outer.close()

    sem_st = ExitStack()
    sem_handles = {s: sem_st.enter_context(nc.semaphore(s)) for s in sorted(mk.sem_names)}
    with nc.Block() as block:
        @block.sync
        def _(sync):
            mk.replay("sp", sync, sem_handles)

        @block.tensor
        def _(tensor):
            mk.replay("pe", tensor, sem_handles)

        @block.scalar
        def _(scalar):
            mk.replay("act", scalar, sem_handles)

        @block.vector
        def _(vector):
            mk.replay("dve", vector, sem_handles)

        @block.gpsimd
        def _(gpsimd):
            mk.replay("pool", gpsimd, sem_handles)
    sem_st.close()
    ps_st.close()
    return nc


def make_in_maps(inputs):
    X = np.ascontiguousarray(np.asarray(inputs["L0_attn_x"], dtype=np.float32).reshape(16384, D))
    wqT = np.ascontiguousarray(np.asarray(inputs["L0_attn_Wq"], np.float32).T)
    wkT = np.ascontiguousarray(np.asarray(inputs["L0_attn_Wk"], np.float32).T)
    wvT = np.ascontiguousarray(np.asarray(inputs["L0_attn_Wv"], np.float32).T)
    woT = np.ascontiguousarray(np.asarray(inputs["L0_attn_Wo"], np.float32).T)
    phiT = np.ascontiguousarray(np.asarray(inputs["L0_ffn_phi_w"], np.float32).T)
    bq = np.asarray(inputs["L0_attn_bq"], np.float32)
    bk = np.asarray(inputs["L0_attn_bk"], np.float32)
    bv = np.asarray(inputs["L0_attn_bv"], np.float32)
    bo = np.asarray(inputs["L0_attn_bo"], np.float32)
    phib = np.asarray(inputs["L0_ffn_phi_b"], np.float32)
    ew = np.asarray(inputs["L0_ffn_expert_w"], np.float32)
    eb = np.asarray(inputs["L0_ffn_expert_b"], np.float32)
    ident = np.eye(128, dtype=np.float32)
    bqf = np.ascontiguousarray(bq.reshape(ND, 128).T)
    bkf = np.ascontiguousarray(bk.reshape(ND, 128).T)
    in_maps = []
    for cid in range(N_CORES):
        t0 = TOK * cid
        xh = np.zeros((512, D), np.float32) if cid == 0 else np.ascontiguousarray(X[t0 - 512:t0])
        kb = np.full((128, 1), (-60.0 - C1) if cid == 0 else -C1, np.float32)
        in_maps.append({
            "x": np.ascontiguousarray(X[t0:t0 + TOK]),
            "xh": xh,
            "wqT": wqT, "wkT": wkT, "wvT": wvT, "woT": woT, "phiT": phiT,
            "bqf": bqf, "bkf": bkf,
            "bvr": np.ascontiguousarray(bv[None, :]),
            "bor": np.ascontiguousarray(bo[None, :]),
            "phibr": np.ascontiguousarray(phib[None, :]),
            "kb": kb,
            "ew": np.ascontiguousarray(ew[EL * cid:EL * cid + EL].reshape(EL * D, D)),
            "eb": np.ascontiguousarray(eb[EL * cid:EL * cid + EL]),
            "ident": ident,
        })
    return in_maps


def kernel(**inputs):
    nc = build()
    in_maps = make_in_maps(inputs)
    res = run_bass_kernel_spmd(nc, in_maps, core_ids=list(range(N_CORES)))
    out = np.concatenate([np.asarray(r["o"], np.float32) for r in res.results], axis=0)
    return out.reshape(1, 16384, D)


# version 49
# speedup vs baseline: 1.0269x; 1.0269x over previous
import numpy as np
from collections import defaultdict
from contextlib import ExitStack

import concourse.bass as bass
import concourse.mybir as mybir
from concourse.bass_utils import run_bass_kernel_spmd

F32 = mybir.dt.float32
BF16 = mybir.dt.bfloat16
AF = mybir.ActivationFunctionType
ALU = mybir.AluOpType
AX = mybir.AxisListType
f32 = np.float32

N_CORES = 8
TOK = 2048
NT = TOK // 128
NCH = TOK // 512
D = 1024
ND = D // 128
H = 16
NP = 2048
NPC = NP // 128
EL = 8
C1 = 3.0
C2 = 4.0
LNEPS = 1e-5
RG = [list(range(N_CORES))]

ENGINES = ("pe", "act", "dve", "pool", "sp")
DMA_POOLS = {"sp": [f"ld{i}" for i in range(6)], "pool": [f"po{i}" for i in range(6)]}
PSUM_BANK = {"ps_cx": 512, "ps_aux": 512, "ps_tr": 1024, "ps_big": 512}


class MKB:

    def __init__(self):
        self.plans = {e: [] for e in ENGINES}
        self.counts = defaultdict(int)
        self.waited = {e: defaultdict(int) for e in ENGINES}
        self.w_recs = defaultdict(list)
        self.r_recs = defaultdict(list)
        self.dma_rr = {"sp": 0, "pool": 0}
        self.pending_barrier = {e: None for e in ENGINES}
        self.sem_names = set(ENGINES) | {s for p in DMA_POOLS.values() for s in p} | {"cc"}

    def _deps(self, eng, reads, writes):
        need = {}

        def add(sem, val, weng):
            if weng == "pe" and eng == "pe":
                return
            if self.waited[eng].get(sem, 0) >= val:
                return
            if need.get(sem, 0) < val:
                need[sem] = val

        def overlap(a, b):
            return a[0] < b[1] and b[0] < a[1] and a[2] < b[3] and b[2] < a[3]

        for r in reads:
            key, rng = r[0], r[1:]
            for rec in self.w_recs[key]:
                if overlap(rng, rec[:4]):
                    add(rec[4], rec[5], rec[6])
        for w in writes:
            key, rng = w[0], w[1:]
            for rec in self.w_recs[key]:
                if overlap(rng, rec[:4]):
                    add(rec[4], rec[5], rec[6])
            for rec in self.r_recs[key]:
                if overlap(rng, rec[:4]):
                    add(rec[4], rec[5], rec[6])
        bar = self.pending_barrier[eng]
        if bar is not None:
            for s, v in bar.items():
                if v > 0 and self.waited[eng].get(s, 0) < v:
                    need[s] = max(need.get(s, 0), v)
            self.pending_barrier[eng] = None
        for s, v in need.items():
            self.waited[eng][s] = max(self.waited[eng].get(s, 0), v)
        return need

    def _record(self, eng, sem, val, reads, writes):
        def covers(a, b):
            return a[0] <= b[0] and a[1] >= b[1] and a[2] <= b[2] and a[3] >= b[3]
        for w in writes:
            key, rng = w[0], w[1:]
            self.w_recs[key] = [x for x in self.w_recs[key] if not covers(rng, x[:4])]
            self.w_recs[key].append((*rng, sem, val, eng))
            self.r_recs[key] = [x for x in self.r_recs[key] if not covers(rng, x[:4])]
        for r in reads:
            key, rng = r[0], r[1:]
            self.r_recs[key] = [x for x in self.r_recs[key] if not covers(rng, x[:4])]
            self.r_recs[key].append((*rng, sem, val, eng))

    @staticmethod
    def _norm(reads, writes):
        def ex(item):
            key, p0, p1, e0, e1 = item
            b = PSUM_BANK.get(key)
            if b:
                e0 = (e0 // b) * b
                e1 = ((e1 + b - 1) // b) * b
                return (key, 0, 128, e0, e1), True
            return (key, p0, p1, e0, e1), False
        nr, nw = [], list()
        for w in writes:
            nw.append(ex(w)[0])
        for r in reads:
            item, is_psum = ex(r)
            (nw if is_psum else nr).append(item)
        return nr, nw

    def op(self, eng, fn, reads=(), writes=()):
        reads, writes = self._norm(reads, writes)
        need = self._deps(eng, reads, writes)
        self.counts[eng] += 1
        v = self.counts[eng]
        self.plans[eng].append((need, fn, eng, 1))
        self._record(eng, eng, v, reads, writes)

    def dma(self, qeng, fn, reads=(), writes=()):
        reads, writes = self._norm(reads, writes)
        pool = DMA_POOLS[qeng]
        sem = pool[self.dma_rr[qeng] % len(pool)]
        self.dma_rr[qeng] += 1
        need = self._deps(qeng, reads, writes)
        prev = self.counts[sem]
        if prev > 0 and self.waited[qeng].get(sem, 0) < prev:
            need[sem] = max(need.get(sem, 0), prev)
            self.waited[qeng][sem] = max(self.waited[qeng].get(sem, 0), prev)
        self.counts[sem] = prev + 16
        self.plans[qeng].append((need, fn, sem, 16))
        self._record("dma", sem, prev + 16, reads, writes)

    def collective(self, fn, reads=(), writes=()):
        need = self._deps("pool", reads, writes)
        self.counts["cc"] += 1
        v = self.counts["cc"]
        self.plans["pool"].append((need, fn, "cc", 1))
        self._record("cc", "cc", v, reads, writes)

    def barrier(self, exclude=()):
        snap = {s: self.counts[s] for s in list(self.counts) if s not in exclude}
        for e in ENGINES:
            pb = self.pending_barrier[e]
            if pb is not None:
                for s, v in snap.items():
                    pb[s] = max(pb.get(s, 0), v)
            else:
                self.pending_barrier[e] = dict(snap)
        if exclude:
            for recs in (self.w_recs, self.r_recs):
                for k in list(recs):
                    kept = [x for x in recs[k] if x[4] in exclude]
                    if kept:
                        recs[k] = kept
                    else:
                        del recs[k]
        else:
            self.w_recs.clear()
            self.r_recs.clear()

    def wait_all(self, eng, sems):
        need = {s: self.counts[s] for s in sems if self.counts[s] > 0}
        self.plans[eng].append((need, None, None, 0))

    def replay(self, eng, handle, sem_handles):
        for need, fn, sem, amt in self.plans[eng]:
            for s in sorted(need):
                handle.wait_ge(sem_handles[s], need[s])
            if fn is not None:
                inst = fn(handle)
                if sem is not None:
                    inst.then_inc(sem_handles[sem], amt)


def build(taps=()):
    nc = bass.Bass("TRN2", target_bir_lowering=False)
    mk = MKB()
    taps = set(taps)

    dx = nc.dram_tensor("x", [TOK, D], F32, kind="ExternalInput")
    dxh = nc.dram_tensor("xh", [512, D], F32, kind="ExternalInput")
    dwq = nc.dram_tensor("wqT", [D, D], F32, kind="ExternalInput")
    dwk = nc.dram_tensor("wkT", [D, D], F32, kind="ExternalInput")
    dwv = nc.dram_tensor("wvT", [D, D], F32, kind="ExternalInput")
    dwo = nc.dram_tensor("woT", [D, D], F32, kind="ExternalInput")
    dphi = nc.dram_tensor("phiT", [D, NP], F32, kind="ExternalInput")
    dbq = nc.dram_tensor("bqf", [128, ND], F32, kind="ExternalInput")
    dbk = nc.dram_tensor("bkf", [128, ND], F32, kind="ExternalInput")
    dbv = nc.dram_tensor("bvr", [1, D], F32, kind="ExternalInput")
    dbo = nc.dram_tensor("bor", [1, D], F32, kind="ExternalInput")
    dphib = nc.dram_tensor("phibr", [1, NP], F32, kind="ExternalInput")
    dkb = nc.dram_tensor("kb", [128, 1], F32, kind="ExternalInput")
    dew = nc.dram_tensor("ew", [EL * D, D], F32, kind="ExternalInput")
    deb = nc.dram_tensor("eb", [EL, D], F32, kind="ExternalInput")
    dident = nc.dram_tensor("ident", [128, 128], F32, kind="ExternalInput")
    do = nc.dram_tensor("o", [TOK, D], F32, kind="ExternalOutput")

    x2_d = nc.dram_tensor("x2_d", [TOK, D], F32, kind="Internal")
    xp_d = nc.dram_tensor("xp_d", [NP, D], F32, kind="Internal")
    xt_d = nc.dram_tensor("xt_d", [NP // N_CORES, D], F32, kind="Internal")
    yg_in = nc.dram_tensor("yg_in", [NP // N_CORES, D], BF16, kind="Internal")
    yg_out = nc.dram_tensor("yg_out", [NP, D], BF16, kind="Internal")
    ctx_d = nc.dram_tensor("ctx_d", [D, TOK], BF16, kind="Internal")

    tap_d = {}

    def mktap(name, shape, dtype=F32):
        if name in taps:
            tap_d[name] = nc.dram_tensor("tap_" + name, shape, dtype, kind="ExternalOutput")

    mktap("xnfm", [128, 4096]); mktap("kf", [128, 4096]); mktap("vt", [128, 4160])
    mktap("qf", [128, 4096]); mktap("pb", [128, 4096]); mktap("ctx", [128, NCH * 4096])
    mktap("x2", [TOK, D]); mktap("E", [128, 2048]); mktap("rP", [128, NT * 64])
    mktap("rN", [128, NT * 32]); mktap("xp", [NP, D]); mktap("xtn", [256, D])
    mktap("ytl", [NP, D])

    def tap(name, buf, p0, p1, e0, e1, dst_r0=0, key=None):
        if name not in taps:
            return
        t = tap_d[name]
        rows = p1 - p0

        def fn(h, buf=buf, t=t, p0=p0, p1=p1, e0=e0, e1=e1, dst_r0=dst_r0):
            return h.dma_start(out=t[dst_r0:dst_r0 + rows, 0:(e1 - e0)], in_=buf[p0:p1, e0:e1])
        mk.dma("pool", fn, reads=[(key or buf.name, p0, p1, e0, e1)],
               writes=[("tap_" + name, dst_r0, dst_r0 + rows, 0, e1 - e0)])

    st_outer = ExitStack()

    def sb(name, shape, dtype, stack, side="right"):
        return stack.enter_context(nc.sbuf_tensor(name, shape, dtype, side=side))

    ident_b = sb("ident_b", [128, 128], BF16, st_outer, "left")
    ones_b = sb("ones_b", [1, 128], BF16, st_outer, "left")
    ident_f = sb("ident_f", [128, 128], F32, st_outer, "left")
    rows_b = sb("rows_b", [1, 2048], BF16, st_outer, "left")
    kb_s = sb("kb_s", [128, 1], F32, st_outer, "left")
    bqf = sb("bqf_s", [128, ND], F32, st_outer, "left")
    bkf = sb("bkf_s", [128, ND], F32, st_outer, "left")
    lns = sb("lns", [128, 48], F32, st_outer, "left")
    rrow = sb("rrow", [1, 1024], BF16, st_outer, "left")
    lnr = sb("lnr", [1, 1024], F32, st_outer, "left")

    ps_st = ExitStack()
    ps_cx = ps_st.enter_context(nc.psum_tensor("ps_cx", [128, 1024], F32))
    ps_aux = ps_st.enter_context(nc.psum_tensor("ps_aux", [128, 512], F32))
    ps_tr = ps_st.enter_context(nc.psum_tensor("ps_tr", [128, 1024], BF16))
    ps_big = ps_st.enter_context(nc.psum_tensor("ps_big", [128, 2048], F32))

    def dma_in(dst, dsl, src, ssl, key, q="sp"):
        def fn(h):
            return h.dma_start(out=dst[dsl[0]:dsl[1], dsl[2]:dsl[3]],
                               in_=src[ssl[0]:ssl[1], ssl[2]:ssl[3]])
        mk.dma(q, fn, reads=[(key, *ssl)], writes=[(dst.name, *dsl)])

    def cdma(dst, dsl, srct, ssl, key):
        def fn(h):
            return h.dma_start(out=dst[dsl[0]:dsl[1], dsl[2]:dsl[3]],
                               in_=srct[ssl[0]:ssl[1], ssl[2]:ssl[3]])
        mk.dma("pool", fn, reads=[(key, *ssl)], writes=[(dst.name, *dsl)])

    def cast(dst, dsl, src, ssl, eng="dve"):
        def fn(h):
            if eng == "act":
                return h.copy(dst[dsl[0]:dsl[1], dsl[2]:dsl[3]],
                              src[ssl[0]:ssl[1], ssl[2]:ssl[3]])
            return h.tensor_copy(dst[dsl[0]:dsl[1], dsl[2]:dsl[3]],
                                 src[ssl[0]:ssl[1], ssl[2]:ssl[3]])
        mk.op(eng, fn, reads=[(src.name, *ssl)], writes=[(dst.name, *dsl)])

    def mm(out, osl, lhs, lsl, rhs, rsl, start, stop):
        def fn(h):
            return h.matmul(out[osl[0]:osl[1], osl[2]:osl[3]],
                            lhs[lsl[0]:lsl[1], lsl[2]:lsl[3]],
                            rhs[rsl[0]:rsl[1], rsl[2]:rsl[3]], start=start, stop=stop)
        mk.op("pe", fn,
              reads=[(lhs.name, *lsl), (rhs.name, *rsl)] +
                    ([(out.name, *osl)] if not start else []),
              writes=[(out.name, *osl)])

    ln_ctr = [0]

    def ln_tile(src, ssl, dst, dsl, rows=128):
        b = (ln_ctr[0] % 8) * 6
        ln_ctr[0] += 1
        c_sum, c_negm, c_ssq, c_std, c_rstd = b, b + 1, b + 2, b + 3, b + 4
        R = rows
        mk.op("dve", lambda h: h.reduce_sum(lns[0:R, c_sum:c_sum + 1], src[ssl[0]:ssl[1], ssl[2]:ssl[3]], axis=AX.X),
              reads=[(src.name, *ssl)], writes=[("lns", 0, 128, c_sum, c_sum + 1)])
        mk.op("dve", lambda h: h.tensor_scalar(lns[0:R, c_negm:c_negm + 1], lns[0:R, c_sum:c_sum + 1], -1.0 / D, None, ALU.mult),
              reads=[("lns", 0, 128, c_sum, c_sum + 1)], writes=[("lns", 0, 128, c_negm, c_negm + 1)])
        mk.op("act", lambda h: h.activation(dst[dsl[0]:dsl[1], dsl[2]:dsl[3]], src[ssl[0]:ssl[1], ssl[2]:ssl[3]],
                                            AF.Square, bias=lns[0:R, c_negm:c_negm + 1], scale=f32(1.0),
                                            accum_out=lns[0:R, c_ssq:c_ssq + 1]),
              reads=[(src.name, *ssl), ("lns", 0, 128, c_negm, c_negm + 1)],
              writes=[(dst.name, *dsl), ("lns", 0, 128, c_ssq, c_ssq + 1)])
        mk.op("act", lambda h: h.activation(lns[0:R, c_std:c_std + 1], lns[0:R, c_ssq:c_ssq + 1], AF.Ln,
                                            bias=f32(LNEPS), scale=f32(1.0 / D)),
              reads=[("lns", 0, 128, c_ssq, c_ssq + 1)], writes=[("lns", 0, 128, c_std, c_std + 1)])
        mk.op("act", lambda h: h.activation(lns[0:R, c_rstd:c_rstd + 1], lns[0:R, c_std:c_std + 1], AF.Exp,
                                            bias=f32(0.0), scale=f32(-0.5)),
              reads=[("lns", 0, 128, c_std, c_std + 1)], writes=[("lns", 0, 128, c_rstd, c_rstd + 1)])
        mk.op("dve", lambda h: h.tensor_scalar(dst[dsl[0]:dsl[1], dsl[2]:dsl[3]],
                                               src[ssl[0]:ssl[1], ssl[2]:ssl[3]],
                                               lns[0:R, c_negm:c_negm + 1], lns[0:R, c_rstd:c_rstd + 1], ALU.add, ALU.mult),
              reads=[(src.name, *ssl), ("lns", 0, 128, c_negm, c_negm + 1), ("lns", 0, 128, c_rstd, c_rstd + 1)],
              writes=[(dst.name, *dsl)])

    dma_in(kb_s, (0, 128, 0, 1), dkb, (0, 128, 0, 1), "dkb")
    dma_in(bqf, (0, 128, 0, ND), dbq, (0, 128, 0, ND), "dbq")
    dma_in(bkf, (0, 128, 0, ND), dbk, (0, 128, 0, ND), "dbk")
    mk.op("dve", lambda h: h.memset(ones_b[0:1, 0:128], 1.0), writes=[("ones_b", 0, 1, 0, 128)])

    st_phi = ExitStack()
    phi = sb("phi", [128, ND * 2048], BF16, st_phi, "left")

    wo0 = sb("wo0", [128, 1024], BF16, st_outer)
    st_A1 = ExitStack()
    xa = sb("xa", [128, 2 * 1024], F32, st_A1)
    ctxu = sb("ctxu", [128, 2 * 512], BF16, st_A1)
    ctxw = sb("ctxw", [64, 2 * 512], BF16, st_A1)
    wq = sb("wq", [128, ND * 1024], BF16, st_A1)
    wk = sb("wk", [128, ND * 1024], BF16, st_A1)
    wv = sb("wv", [128, ND * 1024], BF16, st_A1)
    xn_t = sb("xn_t", [128, 2 * 1024], BF16, st_A1)
    xn_fm = sb("xn_fm", [128, 2 * 4096], BF16, st_A1)
    kf = sb("kf", [128, 3 * 4096], BF16, st_A1)
    vt = sb("vt", [128, 3 * 4160], BF16, st_A1)
    VT_W = 3 * 4160

    cdma(ident_b, (0, 128, 0, 128), dident, (0, 128, 0, 128), "dident")
    dma_in(ident_f, (0, 128, 0, 128), dident, (0, 128, 0, 128), "dident")
    for (dsrc, key, off, n) in ((dbv, "dbv", 0, D), (dbo, "dbo", D, D)):
        cdma(rows_b, (0, 1, off, off + n), dsrc, (0, 1, 0, n), key)
    for w_s, w_d, key in ((wk, dwk, "dwk"), (wv, dwv, "dwv"), (wq, dwq, "dwq")):
        def fwl(h, w_s=w_s, w_d=w_d):
            return h.dma_start(
                out=bass.AP(w_s, 0, [[ND * 1024, 128], [1024, 8], [1, 1024]]),
                in_=bass.AP(w_d, 0, [[1024, 128], [1024 * 128, 8], [1, 1024]]))
        mk.dma("pool", fwl, reads=[(key, 0, D, 0, D)], writes=[(w_s.name, 0, 128, 0, ND * 1024)])

    cdma(wo0, (0, 128, 0, 1024), dwo, (0, 128, 0, D), "dwo")

    def fphi(h):
        return h.dma_start(
            out=bass.AP(phi, 0, [[ND * 2048, 128], [2048, 8], [1, 2048]]),
            in_=bass.AP(dphi, 0, [[2048, 128], [2048 * 128, 8], [1, 2048]]))
    mk.dma("pool", fphi, reads=[("dphi", 0, D, 0, NP)], writes=[("phi", 0, 128, 0, ND * 2048)])
    qf = sb("qf", [128, 2 * 4096], BF16, st_A1)
    pbuf = sb("pbuf", [128, 2 * 4096], BF16, st_A1)

    for slot in range(3):
        for u in range(4):
            base = slot * 4160 + u * 1040

            def fo(h, base=base):
                return h.memset(bass.AP(vt, base + 64, [[VT_W, 128], [65, 16], [1, 1]]), 1.0)
            mk.op("dve", fo, writes=[("vt", 0, 128, base, base + 1040)])

    def xn_load(w, u):
        src_t, skey, r0 = (dxh, "dxh", 0) if w < 0 else (dx, "dx", 512 * w)
        asl = (u % 2) * 1024
        dma_in(xa, (0, 128, asl, asl + 1024), src_t, (r0 + u * 128, r0 + u * 128 + 128, 0, D), skey)

    def xn_ln(w, xslot, u):
        asl = (u % 2) * 1024
        tsl = (u % 2) * 1024
        ln_tile(xa, (0, 128, asl, asl + 1024), xn_t, (0, 128, tsl, tsl + 1024))

    def xn_tr(w, xslot, u):
        tsl = (u % 2) * 1024
        for dd in range(ND):
            def ftr(h, tsl=tsl, dd=dd):
                return h.transpose(ps_tr[0:128, dd * 128:dd * 128 + 128],
                                   xn_t[0:128, tsl + dd * 128:tsl + dd * 128 + 128],
                                   ident_b[0:128, 0:128])
            mk.op("pe", ftr, reads=[("xn_t", 0, 128, tsl + dd * 128, tsl + dd * 128 + 128),
                                    ("ident_b", 0, 128, 0, 128)],
                  writes=[("ps_tr", 0, 128, dd * 128, dd * 128 + 128)])
        XN_W = 2 * 4096

        def fev(h, u=u, xslot=xslot):
            return h.tensor_copy(bass.AP(xn_fm, xslot * 4096 + u * 128, [[XN_W, 128], [512, 8], [1, 128]]),
                                 ps_tr[0:128, 0:1024])
        mk.op("dve", fev, reads=[("ps_tr", 0, 128, 0, 1024)],
              writes=[("xn_fm", 0, 128, xslot * 4096, xslot * 4096 + 4096)])

    def projfm_unit(wt, bias_fm, xslot, dst, dst_off, jc):
        osl = (0, 128, 0, 512)
        for dd in range(ND):
            mm(ps_aux, osl, wt, (0, 128, dd * 1024 + jc * 128, dd * 1024 + jc * 128 + 128),
               xn_fm, (0, 128, xslot * 4096 + dd * 512, xslot * 4096 + dd * 512 + 512),
               start=(dd == 0), stop=(dd == ND - 1))
        dsl = (0, 128, dst_off + jc * 512, dst_off + jc * 512 + 512)

        def fb(h, osl=osl, dsl=dsl, jc=jc, bias_fm=bias_fm, dst=dst):
            return h.tensor_scalar(dst[dsl[0]:dsl[1], dsl[2]:dsl[3]],
                                   ps_aux[osl[0]:osl[1], osl[2]:osl[3]],
                                   bias_fm[0:128, jc:jc + 1], None, ALU.add)
        mk.op("dve", fb, reads=[("ps_aux", *osl), (bias_fm.name, 0, 128, jc, jc + 1)],
              writes=[(dst.name, *dsl)])

    def v_unit(xslot, vslot, u, jh):
        osl = (0, 128, 0, 512)
        for dd in range(ND):
            mm(ps_aux, osl,
               xn_fm, (0, 128, xslot * 4096 + dd * 512 + u * 128,
                       xslot * 4096 + dd * 512 + u * 128 + 128),
               wv, (0, 128, dd * 1024 + jh * 512, dd * 1024 + jh * 512 + 512),
               start=(dd == 0), stop=False)
        mm(ps_aux, osl, ones_b, (0, 1, 0, 128), rows_b, (0, 1, jh * 512, jh * 512 + 512),
           start=False, stop=True)
        base = vslot * 4160 + u * 1040 + 8 * jh * 65

        def fe(h, osl=osl, base=base):
            return h.tensor_copy(bass.AP(vt, base, [[VT_W, 128], [65, 8], [1, 64]]),
                                 ps_aux[osl[0]:osl[1], osl[2]:osl[3]])
        mk.op("dve", fe, reads=[("ps_aux", *osl)],
              writes=[("vt", 0, 128, base, base + 8 * 65)])

    def sgS(c, slot_prev, slot_cur, hp, s):
        heads = (2 * hp, 2 * hp + 1)
        jc = hp
        for h_ in heads:
            hb = 64 * (h_ % 2)
            qoff = (c % 2) * 4096 + jc * 512
            bb = (h_ % 2) * 1024
            for j in range(2):
                kc = 2 * s + j
                slot = slot_prev if kc < 4 else slot_cur
                kk = kc % 4
                mm(ps_big, (0, 128, bb + j * 512, bb + j * 512 + 512),
                   kf, (hb, hb + 64, slot * 4096 + jc * 512 + kk * 128,
                        slot * 4096 + jc * 512 + kk * 128 + 128),
                   qf, (hb, hb + 64, qoff, qoff + 512),
                   start=True, stop=True)
        halo = (c == 0 and s < 2)

        def fx(h, s=s, halo=halo):
            bias = kb_s[0:128, 0:1] if halo else f32(-C1)
            return h.activation(pbuf[0:128, s * 2048:s * 2048 + 2048],
                                ps_big[0:128, 0:2048], AF.Exp, bias=bias, scale=f32(0.125))
        mk.op("act", fx, reads=[("ps_big", 0, 128, 0, 2048)] +
              ([("kb_s", 0, 128, 0, 1)] if halo else []),
              writes=[("pbuf", 0, 128, s * 2048, s * 2048 + 2048)])

    def sgA(c, slot_prev, slot_cur, hp, s):
        for h_ in (2 * hp, 2 * hp + 1):
            for kc in (2 * s, 2 * s + 1):
                slot = slot_prev if kc < 4 else slot_cur
                u = kc % 4
                po = (kc // 2) * 2048 + (h_ % 2) * 1024 + (kc % 2) * 512
                mm(ps_cx, (0, 65, (h_ % 2) * 512, (h_ % 2) * 512 + 512),
                   vt, (0, 128, slot * 4160 + u * 1040 + 65 * h_,
                        slot * 4160 + u * 1040 + 65 * h_ + 65),
                   pbuf, (0, 128, po, po + 512),
                   start=(kc == 0), stop=(kc == 7))

    def attn_norm_a(c, hp):
        for h_ in (2 * hp, 2 * hp + 1):
            hsl = (h_ % 2) * 512

            def fev(h, hsl=hsl):
                return h.tensor_copy(ctxu[0:64, hsl:hsl + 512], ps_cx[0:64, hsl:hsl + 512])
            mk.op("dve", fev, reads=[("ps_cx", 0, 64, hsl, hsl + 512)],
                  writes=[("ctxu", 0, 64, hsl, hsl + 512)])
        mk.op("act", lambda h: h.activation(lnr[0:1, 0:1024], ps_cx[64:65, 0:1024], AF.Ln,
                                            bias=f32(0.0), scale=f32(1.0)),
              reads=[("ps_cx", 64, 65, 0, 1024)], writes=[("lnr", 0, 1, 0, 1024)])
        mk.op("act", lambda h: h.activation(rrow[0:1, 0:1024], lnr[0:1, 0:1024], AF.Exp,
                                            bias=f32(0.0), scale=f32(-1.0)),
              reads=[("lnr", 0, 1, 0, 1024)], writes=[("rrow", 0, 1, 0, 1024)])

    def attn_norm_b(c, hp):
        jc = hp
        for h_ in (2 * hp, 2 * hp + 1):
            hsl = (h_ % 2) * 512
            mm(ps_aux, (0, 64, 0, 512), ones_b, (0, 1, 0, 64), rrow, (0, 1, hsl, hsl + 512),
               start=True, stop=True)

            def fm0(h, hsl=hsl):
                return h.tensor_tensor(ctxw[0:64, hsl:hsl + 512], ctxu[0:64, hsl:hsl + 512],
                                       ps_aux[0:64, 0:512], ALU.mult)
            mk.op("dve", fm0, reads=[("ctxu", 0, 64, hsl, hsl + 512), ("ps_aux", 0, 64, 0, 512)],
                  writes=[("ctxw", 0, 64, hsl, hsl + 512)])
            rb = jc * 128 + (h_ % 2) * 64

            def fd(h, rb=rb, c=c, hsl=hsl):
                return h.dma_start(
                    out=bass.AP(ctx_d, rb * TOK + c * 512, [[TOK, 64], [1, 512]]),
                    in_=ctxw[0:64, hsl:hsl + 512])
            mk.dma("sp", fd, reads=[("ctxw", 0, 64, hsl, hsl + 512)],
                   writes=[("ctx_d", rb, rb + 64, c * 512, c * 512 + 512)])

    def prologue_thunks(w):
        th = []
        xslot = (w + 1) % 2
        kvslot = (w + 1) % 3
        lds_th = [lambda u=u, w=w: xn_load(w, u) for u in range(4)]
        lns_th = [lambda u=u, w=w, xslot=xslot: xn_ln(w, xslot, u) for u in range(4)]
        trs_th = [lambda u=u, w=w, xslot=xslot: xn_tr(w, xslot, u) for u in range(4)]
        th.append(lds_th[0]); th.append(lds_th[1]); th.append(lns_th[0])
        th.append(lds_th[2]); th.append(lns_th[1]); th.append(trs_th[0])
        th.append(lds_th[3]); th.append(lns_th[2]); th.append(trs_th[1])
        th.append(lns_th[3]); th.append(trs_th[2]); th.append(trs_th[3])
        for jc in range(ND):
            th.append(lambda jc=jc, xslot=xslot, kvslot=kvslot:
                      projfm_unit(wk, bkf, xslot, kf, kvslot * 4096, jc))
        for u in range(4):
            for jh in range(2):
                th.append(lambda u=u, jh=jh, xslot=xslot, kvslot=kvslot:
                          v_unit(xslot, kvslot, u, jh))
        if w >= 0:
            for jc in range(ND):
                th.append(lambda jc=jc, xslot=xslot, w=w:
                          projfm_unit(wq, bqf, xslot, qf, (w % 2) * 4096, jc))
        return th

    for t_ in prologue_thunks(-1):
        t_()
    for t_ in prologue_thunks(0):
        t_()
    tap("xnfm", xn_fm, 0, 128, 4096, 8192)
    tap("kf", kf, 0, 128, 4096, 8192)
    tap("vt", vt, 0, 128, 4160, 8320)
    tap("qf", qf, 0, 128, 0, 4096)
    NPAIR = H // 2
    pairs = [(w, hp) for w in range(NCH) for hp in range(NPAIR)]
    thunk_map = {}
    for w in range(NCH - 1):
        th = prologue_thunks(w + 1)
        per = (len(th) + NPAIR - 2) // (NPAIR - 1)
        for j in range(NPAIR - 1):
            thunk_map[(w, j)] = th[j * per:(j + 1) * per]
    tap_pb_after = (0, NPAIR - 1)

    prevarg = None
    for i in range(len(pairs)):
        w, hp = pairs[i]
        arg = (w, w % 3, (w + 1) % 3, hp)
        sgS(*arg, 0)
        if prevarg is not None:
            sgA(*prevarg, 3)
            attn_norm_a(prevarg[0], prevarg[3])
        sgS(*arg, 1)
        sgA(*arg, 0)
        sgS(*arg, 2)
        if prevarg is not None:
            attn_norm_b(prevarg[0], prevarg[3])
        sgA(*arg, 1)
        sgS(*arg, 3)
        sgA(*arg, 2)
        for t_ in thunk_map.get(pairs[i], ()):
            t_()
        if pairs[i] == (0, NPAIR - 1):
            tap("pb", pbuf, 0, 128, 4096, 8192)
        prevarg = arg
    sgA(*prevarg, 3)
    attn_norm_a(prevarg[0], prevarg[3])
    attn_norm_b(prevarg[0], prevarg[3])

    mk.barrier()
    st_A1.close()

    phibrow = sb("phibrow", [1, 2048], BF16, st_phi, "left")
    rP = sb("rP", [128, NT * 64], F32, st_outer)
    rN = sb("rN", [128, NT * 32], F32, st_outer)
    E = sb("E", [128, NT * 2048], BF16, st_outer)
    st_AB = ExitStack()
    xn2t = sb("xn2t", [128, NT * 1024], BF16, st_AB)
    st_A2 = ExitStack()
    wo = sb("wo", [128, ND * 1024], BF16, st_A2)
    x2b = sb("x2b", [128, 2 * 1024], F32, st_A2)
    xn2fm = sb("xn2fm", [128, 2 * 1024], BF16, st_A2)
    ctxr = sb("ctxr", [128, 2 * 1024], BF16, st_A2)
    xa2 = sb("xa2", [128, 2 * 1024], F32, st_A2)
    E_W = NT * 2048
    CR_W = 2 * 1024

    cdma(phibrow, (0, 1, 0, NP), dphib, (0, 1, 0, NP), "dphib")
    for dd in range(1, ND):
        cdma(wo, (0, 128, dd * 1024, dd * 1024 + 1024), dwo,
             (dd * 128, dd * 128 + 128, 0, D), "dwo")

    def ctxr_load(t):
        c, u = t // 4, t % 4
        slot = (t % 2) * 1024

        def fl(h, c=c, u=u, slot=slot):
            return h.dma_start(
                out=bass.AP(ctxr, slot, [[CR_W, 128], [128, 8], [1, 128]]),
                in_=bass.AP(ctx_d, c * 512 + u * 128,
                            [[TOK, 128], [128 * TOK, 8], [1, 128]]))
        mk.dma("sp", fl, reads=[("ctx_d", 0, D, c * 512 + u * 128, c * 512 + u * 128 + 128)],
               writes=[("ctxr", 0, 128, slot, slot + 1024)])

    def xa2_load(t):
        asl = (t % 2) * 1024
        dma_in(xa2, (0, 128, asl, asl + 1024), dx, (t * 128, t * 128 + 128, 0, D), "dx", q="pool")

    def a2_wo(t):
        asl = (t % 2) * 1024
        bsl = (t % 2) * 1024
        slot = (t % 2) * 1024
        for jh in range(2):
            osl = (0, 128, jh * 512, jh * 512 + 512)
            for dd in range(ND):
                wsrc, wb = (wo0, 0) if dd == 0 else (wo, dd * 1024)
                mm(ps_cx, osl, ctxr, (0, 128, slot + dd * 128, slot + dd * 128 + 128),
                   wsrc, (0, 128, wb + jh * 512, wb + jh * 512 + 512),
                   start=(dd == 0), stop=False)
            mm(ps_cx, osl, ones_b, (0, 1, 0, 128), rows_b, (0, 1, D + jh * 512, D + jh * 512 + 512),
               start=False, stop=True)

            def fa(h, jh=jh, asl=asl, bsl=bsl, osl=osl):
                return h.tensor_tensor(x2b[0:128, bsl + jh * 512:bsl + jh * 512 + 512],
                                       xa2[0:128, asl + jh * 512:asl + jh * 512 + 512],
                                       ps_cx[osl[0]:osl[1], osl[2]:osl[3]], ALU.add)
            mk.op("dve", fa, reads=[("xa2", 0, 128, asl + jh * 512, asl + jh * 512 + 512),
                                    ("ps_cx", *osl)],
                  writes=[("x2b", 0, 128, bsl + jh * 512, bsl + jh * 512 + 512)])

        def fw(h, t=t, bsl=bsl):
            return h.dma_start(out=x2_d[t * 128:t * 128 + 128, 0:D], in_=x2b[0:128, bsl:bsl + 1024])
        mk.dma("pool", fw, reads=[("x2b", 0, 128, bsl, bsl + 1024)],
               writes=[("x2_d", t * 128, t * 128 + 128, 0, D)])

    def ab_ln(t):
        bsl = (t % 2) * 1024
        ln_tile(x2b, (0, 128, bsl, bsl + 1024), xn2t, (0, 128, t * 1024, t * 1024 + 1024))

    def ab_tr(t):
        fsl = (t % 2) * 1024
        for dd in range(ND):
            def ftr(h, t=t, dd=dd):
                return h.transpose(ps_tr[0:128, dd * 128:dd * 128 + 128],
                                   xn2t[0:128, t * 1024 + dd * 128:t * 1024 + dd * 128 + 128],
                                   ident_b[0:128, 0:128])
            mk.op("pe", ftr, reads=[("xn2t", 0, 128, t * 1024 + dd * 128, t * 1024 + dd * 128 + 128),
                                    ("ident_b", 0, 128, 0, 128)],
                  writes=[("ps_tr", 0, 128, dd * 128, dd * 128 + 128)])

        def fcv(h, fsl=fsl):
            return h.copy(xn2fm[0:128, fsl:fsl + 1024], ps_tr[0:128, 0:1024])
        mk.op("act", fcv, reads=[("ps_tr", 0, 128, 0, 1024)],
              writes=[("xn2fm", 0, 128, fsl, fsl + 1024)])

    B1SLOTS = ((ps_big, "ps_big", 0), (ps_big, "ps_big", 1024))

    def b_logits(t):
        fsl = (t % 2) * 1024
        for half in range(2):
            pst, pkey, po = B1SLOTS[half]
            for es in (2 * half, 2 * half + 1):
                osl = (0, 128, po + (es % 2) * 512, po + (es % 2) * 512 + 512)
                for dd in range(ND):
                    mm(pst, osl, xn2fm, (0, 128, fsl + dd * 128, fsl + dd * 128 + 128),
                       phi, (0, 128, dd * 2048 + es * 512, dd * 2048 + es * 512 + 512),
                       start=(dd == 0), stop=False)
                mm(pst, osl, ones_b, (0, 1, 0, 128),
                   phibrow, (0, 1, es * 512, es * 512 + 512), start=False, stop=True)

        def fx(h, t=t):
            return h.activation(E[0:128, t * 2048:t * 2048 + 2048],
                                ps_big[0:128, 0:2048],
                                AF.Exp, bias=f32(-C2), scale=f32(1.0))
        mk.op("act", fx, reads=[("ps_big", 0, 128, 0, 2048)],
              writes=[("E", 0, 128, t * 2048, t * 2048 + 2048)])

    def b_reduce(t):
        def fp(h, t=t):
            return h.reduce_sum(rP[0:128, t * 64:t * 64 + 64],
                                bass.AP(E, t * 2048, [[E_W, 128], [32, 64], [1, 32]]), axis=AX.X)
        mk.op("dve", fp, reads=[("E", 0, 128, t * 2048, t * 2048 + 2048)],
              writes=[("rP", 0, 128, t * 64, t * 64 + 64)])
        mk.op("dve", lambda h, t=t: h.reciprocal(rP[0:128, t * 64:t * 64 + 64], rP[0:128, t * 64:t * 64 + 64]),
              reads=[("rP", 0, 128, t * 64, t * 64 + 64)], writes=[("rP", 0, 128, t * 64, t * 64 + 64)])

        def fn_(h, t=t):
            return h.reduce_sum(rN[0:128, t * 32:t * 32 + 32],
                                bass.AP(E, t * 2048, [[E_W, 128], [1, 32], [32, 64]]), axis=AX.X)
        mk.op("dve", fn_, reads=[("E", 0, 128, t * 2048, t * 2048 + 2048)],
              writes=[("rN", 0, 128, t * 32, t * 32 + 32)])
        mk.op("dve", lambda h, t=t: h.reciprocal(rN[0:128, t * 32:t * 32 + 32], rN[0:128, t * 32:t * 32 + 32]),
              reads=[("rN", 0, 128, t * 32, t * 32 + 32)], writes=[("rN", 0, 128, t * 32, t * 32 + 32)])

    ctxr_load(0)
    xa2_load(0)
    for i in range(NT + 2):
        if i < NT:
            if i + 1 < NT:
                ctxr_load(i + 1)
                xa2_load(i + 1)
            a2_wo(i)
            ab_ln(i)
        if 1 <= i <= NT:
            ab_tr(i - 1)
        if i >= 2:
            b_logits(i - 2)
            b_reduce(i - 2)
    tap("E", E, 0, 128, 0, 2048)
    tap("rP", rP, 0, 128, 0, NT * 64)
    tap("rN", rN, 0, 128, 0, NT * 32)
    if "x2" in taps:
        for t in range(NT):
            def fxt(h, t=t):
                return h.dma_start(out=tap_d["x2"][t * 128:t * 128 + 128, 0:D],
                                   in_=x2_d[t * 128:t * 128 + 128, 0:D])
            mk.dma("pool", fxt, reads=[("x2_d", t * 128, t * 128 + 128, 0, D)],
                   writes=[("tap_x2", t * 128, t * 128 + 128, 0, D)])

    mk.barrier()
    st_A2.close()
    st_phi.close()

    st_Y = ExitStack()
    ytlE = sb("ytlE", [128, (NPC // 2) * 1024], BF16, st_Y)
    st_BC = ExitStack()
    dslab = sb("dslab", [128, 2 * NT * 128], BF16, st_BC)
    rpx = sb("rpx", [128, 2 * 128], BF16, st_BC)
    xps = sb("xps", [128, 2 * 1024], F32, st_BC)
    xtb = sb("xtb", [128, 2 * 1024], F32, st_BC)
    xtn = sb("xtn", [128, 2 * 1024], BF16, st_BC)
    xtT = sb("xtT", [128, ND * 256], BF16, st_BC)
    EWR = 16
    ewb = sb("ewb", [128, EWR * 1024], BF16, st_BC)
    ebrow = sb("ebrow", [1, 4 * 1024], BF16, st_BC)
    ytlo = sb("ytlo", [32, 2 * 1024], BF16, st_BC)

    def ew_load(k):
        esl = (k % EWR) * 1024
        cdma(ewb, (0, 128, esl, esl + 1024), dew, (k * 128, k * 128 + 128, 0, D), "dew")

    def xpart_npc(npc):
        dsl0 = (npc % 2) * (NT * 128)

        def fdm(h, npc=npc, dsl0=dsl0):
            return h.tensor_tensor(
                bass.AP(dslab, dsl0, [[2 * NT * 128, 128], [128, 16], [1, 128]]),
                bass.AP(E, npc * 128, [[E_W, 128], [2048, 16], [1, 128]]),
                bass.AP(rP, npc * 4, [[NT * 64, 128], [64, 16], [1, 4], [0, 32]]),
                ALU.mult)
        mk.op("dve", fdm, reads=[("E", 0, 128, 0, E_W), ("rP", 0, 128, 0, NT * 64)],
              writes=[("dslab", 0, 128, dsl0, dsl0 + NT * 128)])
        pst, po = (ps_cx, 0) if (npc // 2) % 2 == 0 else (ps_big, 1024)
        for dh_ in range(2):
            osl = (0, 128, po + dh_ * 512, po + dh_ * 512 + 512)
            for t in range(NT):
                mm(pst, osl, dslab, (0, 128, dsl0 + t * 128, dsl0 + t * 128 + 128),
                   xn2t, (0, 128, t * 1024 + dh_ * 512, t * 1024 + dh_ * 512 + 512),
                   start=(t == 0), stop=(t == NT - 1))
            cast(xps, (0, 128, (npc % 2) * 1024 + dh_ * 512, (npc % 2) * 1024 + dh_ * 512 + 512),
                 pst, (0, 128, osl[2], osl[3]), eng="act")
        r0 = 1024 * (npc % 2) + 128 * (npc // 2)

        def fxp(h, npc=npc, r0=r0):
            return h.dma_start(out=xp_d[r0:r0 + 128, 0:D],
                               in_=xps[0:128, (npc % 2) * 1024:(npc % 2) * 1024 + 1024])
        mk.dma("pool", fxp, reads=[("xps", 0, 128, (npc % 2) * 1024, (npc % 2) * 1024 + 1024)],
               writes=[("xp_d", r0, r0 + 128, 0, D)])

    def rs_slab(G):
        def frs(h, G=G):
            return h.collective_compute("ReduceScatter", ALU.add, replica_groups=RG,
                                        ins=[xp_d[1024 * G:1024 * G + 1024, 0:D].opt()],
                                        outs=[xt_d[128 * G:128 * G + 128, 0:D].opt()])
        mk.collective(frs, reads=[("xp_d", 1024 * G, 1024 * G + 1024, 0, D)],
                      writes=[("xt_d", 128 * G, 128 * G + 128, 0, D)])

    def xtproc(G):
        xsl = G * 1024

        def fxt(h, G=G, xsl=xsl):
            return h.dma_start(out=xtb[0:128, xsl:xsl + 1024], in_=xt_d[128 * G:128 * G + 128, 0:D])
        mk.dma("sp", fxt, reads=[("xt_d", 128 * G, 128 * G + 128, 0, D)],
               writes=[("xtb", 0, 128, xsl, xsl + 1024)])

        def fcl(h, xsl=xsl):
            return h.tensor_scalar(xtb[0:128, xsl:xsl + 1024], xtb[0:128, xsl:xsl + 1024],
                                   -33000.0, 65000.0, ALU.max, ALU.min)
        mk.op("dve", fcl, reads=[("xtb", 0, 128, xsl, xsl + 1024)],
              writes=[("xtb", 0, 128, xsl, xsl + 1024)])
        ln_tile(xtb, (0, 128, xsl, xsl + 1024), xtn, (0, 128, xsl, xsl + 1024))

        def ftn(h, xsl=xsl):
            return h.activation(xtn[0:128, xsl:xsl + 1024], xtn[0:128, xsl:xsl + 1024],
                                AF.Tanh, bias=f32(0.0), scale=f32(1.0))
        mk.op("act", ftn, reads=[("xtn", 0, 128, xsl, xsl + 1024)],
              writes=[("xtn", 0, 128, xsl, xsl + 1024)])
        for dd in range(ND):
            def ftr(h, dd=dd, xsl=xsl):
                return h.transpose(ps_tr[0:128, dd * 128:dd * 128 + 128],
                                   xtn[0:128, xsl + dd * 128:xsl + dd * 128 + 128],
                                   ident_b[0:128, 0:128])
            mk.op("pe", ftr, reads=[("xtn", 0, 128, xsl + dd * 128, xsl + dd * 128 + 128),
                                    ("ident_b", 0, 128, 0, 128)],
                  writes=[("ps_tr", 0, 128, dd * 128, dd * 128 + 128)])

        def fevT(h, G=G):
            return h.tensor_copy(bass.AP(xtT, 128 * G, [[ND * 256, 128], [256, 8], [1, 128]]),
                                 ps_tr[0:128, 0:1024])
        mk.op("dve", fevT, reads=[("ps_tr", 0, 128, 0, 1024)],
              writes=[("xtT", 0, 128, 0, ND * 256)])
        if "xtn" in taps:
            tap("xtn", xtn, 0, 128, xsl, xsl + 1024, dst_r0=128 * G)

    def eb_load(n):
        esl = (n % 4) * 1024
        cdma(ebrow, (0, 1, esl, esl + 1024), deb, (n, n + 1, 0, D), "deb")

    def expert_unit(n):
        esl = (n % 4) * 1024
        if n + 2 < EL:
            eb_load(n + 2)
        for dd in range(ND):
            k = n * ND + dd
            wsl = (k % EWR) * 1024
            for hh in range(2):
                pst, psl = (ps_aux, (0, 32, 0, 512)) if hh == 0 else (ps_big, (0, 32, 0, 512))
                mm(pst, psl, xtT, (0, 128, dd * 256 + 32 * n, dd * 256 + 32 * n + 32),
                   ewb, (0, 128, wsl + hh * 512, wsl + hh * 512 + 512),
                   start=(dd == 0), stop=False)
            if k + EWR < EL * ND:
                ew_load(k + EWR)
        for hh in range(2):
            pst = ps_aux if hh == 0 else ps_big
            psl = (0, 32, 0, 512)
            mm(pst, psl, ones_b, (0, 1, 0, 32), ebrow, (0, 1, esl + hh * 512, esl + hh * 512 + 512),
               start=False, stop=True)

            def fy(h, n=n, hh=hh, pst=pst):
                return h.copy(ytlo[0:32, (n % 2) * 1024 + hh * 512:(n % 2) * 1024 + hh * 512 + 512],
                              pst[0:32, 0:512])
            mk.op("act", fy, reads=[(pst.name, 0, 32, 0, 512)],
                  writes=[("ytlo", 0, 32, (n % 2) * 1024 + hh * 512, (n % 2) * 1024 + hh * 512 + 512)])

        def fg(h, n=n):
            return h.dma_start(out=yg_in[32 * n:32 * n + 32, 0:D],
                               in_=ytlo[0:32, (n % 2) * 1024:(n % 2) * 1024 + 1024])
        mk.dma("pool", fg, reads=[("ytlo", 0, 32, (n % 2) * 1024, (n % 2) * 1024 + 1024)],
               writes=[("yg_in", 32 * n, 32 * n + 32, 0, D)])

    ew_k = [0]

    def emit_ew(nmax):
        while ew_k[0] < min(nmax, EWR):
            ew_load(ew_k[0])
            ew_k[0] += 1

    evens = [npc for npc in range(NPC) if npc % 2 == 0]
    odds = [npc for npc in range(NPC) if npc % 2 == 1]
    for i, npc in enumerate(evens):
        xpart_npc(npc)
        emit_ew((i + 1) * 2)
    rs_slab(0)
    for i, npc in enumerate(odds):
        xpart_npc(npc)
        if i == 4:
            xtproc(0)
            eb_load(0)
            eb_load(1)
        if i in (5, 6, 7):
            expert_unit(i - 5)
    rs_slab(1)
    expert_unit(3)
    if "xp" in taps:
        for g in range(2):
            def ftp(h, g=g):
                return h.dma_start(out=tap_d["xp"][1024 * g:1024 * g + 1024, 0:D],
                                   in_=xp_d[1024 * g:1024 * g + 1024, 0:D])
            mk.dma("pool", ftp, reads=[("xp_d", 1024 * g, 1024 * g + 1024, 0, D)],
                   writes=[("tap_xp", 1024 * g, 1024 * g + 1024, 0, D)])

    def fag1(h):
        return h.collective_compute("AllGather", ALU.bypass, replica_groups=RG,
                                    ins=[yg_in[0:128, 0:D].opt()], outs=[yg_out[0:1024, 0:D].opt()])

    def fag2(h):
        return h.collective_compute("AllGather", ALU.bypass, replica_groups=RG,
                                    ins=[yg_in[128:256, 0:D].opt()], outs=[yg_out[1024:2048, 0:D].opt()])

    mk.collective(fag1, reads=[("yg_in", 0, 128, 0, D)], writes=[("yg_out", 0, 1024, 0, D)])
    xtproc(1)
    for npc in range(0, NPC, 2):
        r = 128 * (npc // 2)
        dma_in(ytlE, (0, 128, r * 8, r * 8 + 1024), yg_out, (r, r + 128, 0, D), "yg_out",
               q=("sp" if (npc // 2) % 2 == 0 else "pool"))
    for n in range(4, EL):
        expert_unit(n)
    mk.collective(fag2, reads=[("yg_in", 128, 256, 0, D)], writes=[("yg_out", 1024, 2048, 0, D)])

    mk.barrier(exclude=("cc",))
    st_BC.close()

    st_C = ExitStack()
    ytlO = sb("ytlO", [128, (NPC // 2) * 1024], BF16, st_C)
    rnx = sb("rnx", [128, 3 * 128], BF16, st_C)
    cbuf = sb("cbuf", [128, 2 * 128], F32, st_C)
    ctb = sb("ctb", [128, 3 * 2048], BF16, st_C)
    obuf = sb("obuf", [128, 2 * 1024], F32, st_C)
    x2c = sb("x2c", [128, 4 * 1024], F32, st_C)

    def ytl_row(npc):
        c_ = npc // 2
        return (1024 if npc % 2 else 0) + 128 * c_

    YORD = [npc for npc in range(NPC) if npc % 2 == 0] + [npc for npc in range(NPC) if npc % 2 == 1]
    for npc in range(1, NPC, 2):
        r = ytl_row(npc)
        dma_in(ytlO, (0, 128, (npc // 2) * 1024, (npc // 2) * 1024 + 1024),
               yg_out, (r, r + 128, 0, D), "yg_out",
               q=("sp" if (npc // 2) % 2 == 0 else "pool"))

    def ytl_sl(npc):
        return (ytlE, (npc // 2) * 1024) if npc % 2 == 0 else (ytlO, (npc // 2) * 1024)
    if "ytl" in taps:
        for npc in range(NPC):
            yb, yo = ytl_sl(npc)
            tap("ytl", yb, 0, 128, yo, yo + 1024, dst_r0=npc * 128)

    for tp_ in range(2):
        dma_in(x2c, (0, 128, tp_ * 1024, tp_ * 1024 + 1024), x2_d,
               (tp_ * 128, tp_ * 128 + 128, 0, D), "x2_d")

    def y_rnx(t):
        rsl = (t % 3) * 128

        def frn(h, t=t, rsl=rsl):
            return h.tensor_copy(bass.AP(rnx, rsl, [[3 * 128, 128], [32, 4], [1, 32]]),
                                 bass.AP(rN, t * 32, [[NT * 32, 128], [0, 4], [1, 32]]))
        mk.op("dve", frn, reads=[("rN", 0, 128, t * 32, t * 32 + 32)],
              writes=[("rnx", 0, 128, rsl, rsl + 128)])

    def y_prep_i(t, i):
        rsl = (t % 3) * 128
        npc = YORD[i]
        csl = (i % 2) * 128
        pst, pkey = (ps_aux, "ps_aux") if i % 2 == 0 else (ps_big, "ps_big")

        def fc(h, t=t, npc=npc, csl=csl, rsl=rsl):
            return h.tensor_tensor(cbuf[0:128, csl:csl + 128],
                                   E[0:128, t * 2048 + npc * 128:t * 2048 + npc * 128 + 128],
                                   rnx[0:128, rsl:rsl + 128], ALU.mult)
        mk.op("dve", fc, reads=[("E", 0, 128, t * 2048 + npc * 128, t * 2048 + npc * 128 + 128),
                                ("rnx", 0, 128, rsl, rsl + 128)],
              writes=[("cbuf", 0, 128, csl, csl + 128)])

        def ft(h, csl=csl, pst=pst):
            return h.transpose(pst[0:128, 0:128], cbuf[0:128, csl:csl + 128],
                               ident_f[0:128, 0:128])
        mk.op("pe", ft, reads=[("cbuf", 0, 128, csl, csl + 128), ("ident_f", 0, 128, 0, 128)],
              writes=[(pkey, 0, 128, 0, 128)])
        dsl = (t % 3) * 2048 + i * 128

        def fe(h, dsl=dsl, pst=pst):
            return h.copy(ctb[0:128, dsl:dsl + 128], pst[0:128, 0:128])
        mk.op("act", fe, reads=[(pkey, 0, 128, 0, 128)],
              writes=[("ctb", 0, 128, dsl, dsl + 128)])

    def y_mm(t):
        xsl = (t % 4) * 1024
        if t + 2 < NT:
            dma_in(x2c, (0, 128, ((t + 2) % 4) * 1024, ((t + 2) % 4) * 1024 + 1024), x2_d,
                   ((t + 2) * 128, (t + 2) * 128 + 128, 0, D), "x2_d")
            y_rnx(t + 2)
        for i in range(NPC):
            npc = YORD[i]
            dsl = (t % 3) * 2048 + i * 128
            yb, yo = ytl_sl(npc)
            for hh in range(2):
                mm(ps_cx, (0, 128, hh * 512, hh * 512 + 512), ctb, (0, 128, dsl, dsl + 128),
                   yb, (0, 128, yo + hh * 512, yo + hh * 512 + 512),
                   start=(i == 0), stop=(i == NPC - 1))
            if t + 2 < NT:
                y_prep_i(t + 2, i)
        osl = (t % 2) * 1024
        for hh in range(2):
            def fo(h, hh=hh, osl=osl, xsl=xsl):
                return h.tensor_tensor(obuf[0:128, osl + hh * 512:osl + hh * 512 + 512],
                                       x2c[0:128, xsl + hh * 512:xsl + hh * 512 + 512],
                                       ps_cx[0:128, hh * 512:hh * 512 + 512], ALU.add)
            mk.op("dve", fo, reads=[("x2c", 0, 128, xsl + hh * 512, xsl + hh * 512 + 512),
                                    ("ps_cx", 0, 128, hh * 512, hh * 512 + 512)],
                  writes=[("obuf", 0, 128, osl + hh * 512, osl + hh * 512 + 512)])

        def fod(h, t=t, osl=osl):
            return h.dma_start(out=do[t * 128:t * 128 + 128, 0:D], in_=obuf[0:128, osl:osl + 1024])
        mk.dma("pool", fod, reads=[("obuf", 0, 128, osl, osl + 1024)],
               writes=[("do", t * 128, t * 128 + 128, 0, D)])

    for t_ in range(2):
        y_rnx(t_)
        for i_ in range(NPC):
            y_prep_i(t_, i_)
    for t in range(NT):
        y_mm(t)
    mk.wait_all("pool", DMA_POOLS["pool"] + DMA_POOLS["sp"] + ["cc"])

    st_C.close()
    st_Y.close()
    st_AB.close()
    st_outer.close()

    sem_st = ExitStack()
    sem_handles = {s: sem_st.enter_context(nc.semaphore(s)) for s in sorted(mk.sem_names)}
    with nc.Block() as block:
        @block.sync
        def _(sync):
            mk.replay("sp", sync, sem_handles)

        @block.tensor
        def _(tensor):
            mk.replay("pe", tensor, sem_handles)

        @block.scalar
        def _(scalar):
            mk.replay("act", scalar, sem_handles)

        @block.vector
        def _(vector):
            mk.replay("dve", vector, sem_handles)

        @block.gpsimd
        def _(gpsimd):
            mk.replay("pool", gpsimd, sem_handles)
    sem_st.close()
    ps_st.close()
    return nc


def make_in_maps(inputs):
    X = np.ascontiguousarray(np.asarray(inputs["L0_attn_x"], dtype=np.float32).reshape(16384, D))
    wqT = np.ascontiguousarray(np.asarray(inputs["L0_attn_Wq"], np.float32).T)
    wkT = np.ascontiguousarray(np.asarray(inputs["L0_attn_Wk"], np.float32).T)
    wvT = np.ascontiguousarray(np.asarray(inputs["L0_attn_Wv"], np.float32).T)
    woT = np.ascontiguousarray(np.asarray(inputs["L0_attn_Wo"], np.float32).T)
    phiT = np.ascontiguousarray(np.asarray(inputs["L0_ffn_phi_w"], np.float32).T)
    bq = np.asarray(inputs["L0_attn_bq"], np.float32)
    bk = np.asarray(inputs["L0_attn_bk"], np.float32)
    bv = np.asarray(inputs["L0_attn_bv"], np.float32)
    bo = np.asarray(inputs["L0_attn_bo"], np.float32)
    phib = np.asarray(inputs["L0_ffn_phi_b"], np.float32)
    ew = np.asarray(inputs["L0_ffn_expert_w"], np.float32)
    eb = np.asarray(inputs["L0_ffn_expert_b"], np.float32)
    ident = np.eye(128, dtype=np.float32)
    bqf = np.ascontiguousarray(bq.reshape(ND, 128).T)
    bkf = np.ascontiguousarray(bk.reshape(ND, 128).T)
    in_maps = []
    for cid in range(N_CORES):
        t0 = TOK * cid
        xh = np.zeros((512, D), np.float32) if cid == 0 else np.ascontiguousarray(X[t0 - 512:t0])
        kb = np.full((128, 1), (-60.0 - C1) if cid == 0 else -C1, np.float32)
        in_maps.append({
            "x": np.ascontiguousarray(X[t0:t0 + TOK]),
            "xh": xh,
            "wqT": wqT, "wkT": wkT, "wvT": wvT, "woT": woT, "phiT": phiT,
            "bqf": bqf, "bkf": bkf,
            "bvr": np.ascontiguousarray(bv[None, :]),
            "bor": np.ascontiguousarray(bo[None, :]),
            "phibr": np.ascontiguousarray(phib[None, :]),
            "kb": kb,
            "ew": np.ascontiguousarray(ew[EL * cid:EL * cid + EL].reshape(EL * D, D)),
            "eb": np.ascontiguousarray(eb[EL * cid:EL * cid + EL]),
            "ident": ident,
        })
    return in_maps


def kernel(**inputs):
    nc = build()
    in_maps = make_in_maps(inputs)
    res = run_bass_kernel_spmd(nc, in_maps, core_ids=list(range(N_CORES)))
    out = np.concatenate([np.asarray(r["o"], np.float32) for r in res.results], axis=0)
    return out.reshape(1, 16384, D)
